# version 39
# baseline (speedup 1.0000x reference)
import os
import numpy as np
import concourse.bass as bass
import concourse.mybir as mybir
from concourse.bass_utils import run_bass_kernel_spmd
from contextlib import ExitStack

F32 = mybir.dt.float32
BF16 = mybir.dt.bfloat16
I32 = mybir.dt.int32
I16 = mybir.dt.int16
AF = mybir.ActivationFunctionType
ALU = mybir.AluOpType

NCORES = 8
SEQ = 8192
D = 1024
PF = 256
T = 128
NT = SEQ // T
EPS_LN = 1e-5
EPS_Z = 1e-8
PAR = 3
PARANOID = True

ENGS = ("V", "S", "G", "T", "Y")


class _Op:
    __slots__ = ("engine", "emit", "waits", "semval", "is_dma", "sem_key",
                 "idx", "cost", "lat")

    def __init__(self, engine, emit, waits, is_dma, sem_key, idx, cost, lat):
        self.engine = engine
        self.emit = emit
        self.waits = waits
        self.semval = None
        self.is_dma = is_dma
        self.sem_key = sem_key
        self.idx = idx
        self.cost = cost
        self.lat = lat


class _Buf:
    __slots__ = ("name", "last_write", "readers")

    def __init__(self, name):
        self.name = name
        self.last_write = None
        self.readers = []


class Plan:
    def __init__(self):
        self.ops = {e: [] for e in ENGS}
        self.bufs = {}
        self.nops = 0

    def buf(self, name):
        b = self.bufs.get(name)
        if b is None:
            b = _Buf(name)
            self.bufs[name] = b
        return b

    def op(self, engine, emit, reads=(), writes=(), extra=(), sem_key=None,
           cost=None, lat=None, dma=False):
        waits = set()
        for b in reads:
            if b.last_write is not None:
                waits.add(b.last_write)
        for b in writes:
            waits.update(b.readers)
            if b.last_write is not None:
                waits.add(b.last_write)
        for o in extra:
            if o is not None:
                waits.add(o)
        is_dma = dma or engine == "Y"
        if cost is None:
            cost = 300 if is_dma else 400
        if lat is None:
            lat = 2500 if is_dma else 200
        o = _Op(engine, emit, waits, is_dma, sem_key or engine,
                self.nops, cost, lat)
        self.nops += 1
        self.ops[engine].append(o)
        for b in reads:
            b.readers.append(o)
        for b in writes:
            b.last_write = o
            b.readers = []
        return o

    def schedule(self):
        import heapq
        allops = sorted((o for el in self.ops.values() for o in el),
                        key=lambda o: o.idx)
        succs = {o: [] for o in allops}
        indeg = {o: 0 for o in allops}
        for o in allops:
            for w in o.waits:
                succs[w].append(o)
                indeg[o] += 1
        pr = {}
        for o in reversed(allops):
            m = 0
            for t in succs[o]:
                if pr[t] > m:
                    m = pr[t]
            pr[o] = m + o.cost + o.lat
        engine_free = {e: 0 for e in self.ops}
        ready = {e: [] for e in self.ops}
        completions = []
        order = {e: [] for e in self.ops}
        import os as _os
        sched_engs = set(_os.environ.get("KENG", "VSGTY"))

        def _key(o):
            if o.engine[0] not in sched_engs:
                return (o.idx, o.idx, o)
            return (-pr[o], o.idx, o)
        for o in allops:
            if indeg[o] == 0:
                heapq.heappush(ready[o.engine], _key(o))
        ndone = 0
        now = 0
        while ndone < len(allops):
            progressed = True
            while progressed:
                progressed = False
                for e in self.ops:
                    if engine_free[e] <= now and ready[e]:
                        _, _, o = heapq.heappop(ready[e])
                        start = now
                        engine_free[e] = start + o.cost
                        heapq.heappush(completions,
                                       (start + o.cost + o.lat, o.idx, o))
                        order[e].append(o)
                        ndone += 1
                        progressed = True
            cand = []
            if completions:
                cand.append(completions[0][0])
            for e in self.ops:
                if ready[e] and engine_free[e] > now:
                    cand.append(engine_free[e])
            if not cand:
                if not completions:
                    break
                cand.append(completions[0][0])
            now = min(cand)
            while completions and completions[0][0] <= now:
                _, _, o = heapq.heappop(completions)
                for t in succs[o]:
                    indeg[t] -= 1
                    if indeg[t] == 0:
                        heapq.heappush(ready[t.engine], _key(t))
        assert ndone == len(allops), f"scheduler stuck: {ndone}/{len(allops)}"
        self.ops = order
        return now

    def _needs_sem(self, producer, consumer_engine):
        if producer.engine == "T" and consumer_engine == "T":
            return False
        return (producer.engine != consumer_engine or producer.is_dma
                or PARANOID)

    def transitive_reduce(self):
        spos = {}
        for e, el in self.ops.items():
            for k, o in enumerate(el):
                spos[o] = k
        order = []
        for el in self.ops.values():
            order.extend(el)
        indeg = {o: 0 for o in order}
        succs = {o: [] for o in order}
        for o in order:
            for w in o.waits:
                succs[w].append(o)
                indeg[o] += 1
        from collections import deque
        q = deque(o for o in order if indeg[o] == 0)
        wm = {}
        topo = []
        while q:
            o = q.popleft()
            topo.append(o)
            for t in succs[o]:
                indeg[t] -= 1
                if indeg[t] == 0:
                    q.append(t)
        assert len(topo) == len(order), "reduce topo stuck"
        for o in topo:
            m = {}
            kept = []
            for w in sorted(o.waits, key=lambda w: -spos[w]):
                lane = (w.engine, w.sem_key)
                if m.get(lane, -1) >= spos[w]:
                    continue
                kept.append(w)
                wmw = wm.get(w)
                if wmw:
                    for k2, v2 in wmw.items():
                        if m.get(k2, -1) < v2:
                            m[k2] = v2
                if m.get(lane, -1) < spos[w]:
                    m[lane] = spos[w]
            o.waits = set(kept)
            mo = dict(m)
            mo[(o.engine, o.sem_key)] = spos[o]
            wm[o] = mo

    def finalize(self):
        self.transitive_reduce()
        needed = set()
        for elist in self.ops.values():
            for o in elist:
                for w in o.waits:
                    if self._needs_sem(w, o.engine):
                        needed.add(w)
        ctr = {}
        for elist in self.ops.values():
            for o in elist:
                if o in needed or o.is_dma:
                    v = ctr.get(o.sem_key, 0) + (16 if o.is_dma else 1)
                    ctr[o.sem_key] = v
                    o.semval = v

    def emit_engine(self, ename, eng, sems):
        waited = {}
        for o in self.ops[ename]:
            by = {}
            for w in o.waits:
                if w.semval is None or not self._needs_sem(w, ename):
                    continue
                if by.get(w.sem_key, 0) < w.semval:
                    by[w.sem_key] = w.semval
            for k2 in sorted(by):
                v = by[k2]
                if waited.get(k2, 0) >= v:
                    continue
                eng.wait_ge(sems[k2], v)
                waited[k2] = v
            inst = o.emit(eng)
            if o.semval is not None:
                inst.then_inc(sems[o.sem_key], 16 if o.is_dma else 1)


def build_nc(nt=NT):
    nc = bass.Bass("TRN2", target_bir_lowering=False, debug=False)
    seq = nt * T
    x_d = nc.dram_tensor("x", [seq, D], F32, kind="ExternalInput").ap()
    wq_d = nc.dram_tensor("wq", [D, D], F32, kind="ExternalInput").ap()
    wk_d = nc.dram_tensor("wk", [D, D], F32, kind="ExternalInput").ap()
    wv_d = nc.dram_tensor("wv", [D, D], F32, kind="ExternalInput").ap()
    pj_d = nc.dram_tensor("proj", [D, PF], F32, kind="ExternalInput").ap()
    wf_d = nc.dram_tensor("wf", [D, D], F32, kind="ExternalInput").ap()
    bi_d = nc.dram_tensor("bias", [1, D], F32, kind="ExternalInput").ap()
    out_d = nc.dram_tensor("out", [seq, D], F32, kind="ExternalOutput").ap()
    spill_d = nc.dram_tensor("spill", [seq, D], F32, kind="Internal").ap()
    xn2sp_d = nc.dram_tensor("xn2sp", [seq, D], BF16, kind="Internal").ap()

    pl = Plan()
    B = pl.buf

    with ExitStack() as ctx:
        def sb(name, shape, dt=F32):
            return ctx.enter_context(nc.sbuf_tensor(name, shape, dt))

        def psum(name, shape, dt=F32):
            return ctx.enter_context(nc.psum_tensor(name, shape, dt))

        wqT = sb("wqT", [T, 8 * D], BF16)
        Wqkp = sb("Wqkp", [T, 8 * 2 * PF], BF16)
        WvT = sb("WvT", [T, 8 * D], BF16)
        biash = sb("biash", [T, D], F32)
        qptm = sb("qptm", [T, max(PF * nt, 8 * D) + 4 * 2 * T], BF16)
        qpfr_off = max(PF * nt, 8 * D)
        zst = sb("zst", [T, nt], F32)
        kvsb = sb("kvsb", [T, 2 * D], BF16)
        identb = sb("identb", [T, T], BF16)
        onesf = sb("onesf", [1, T], F32)
        ceps = sb("ceps", [T, 1], F32)

        def pair(name, shape, dt=F32, d=None):
            return [sb(f"{name}{i}", shape, dt) for i in range(d or PAR)]

        SIMFIX = os.environ.get("BASS_SIM_FP64", "0") == "1"
        XBD = 2 if SIMFIX else max(PAR, 4)
        xb = pair("xb", [T, D], d=XBD)
        xc = pair("xc", [T, D])
        xd = pair("xd", [T, D], d=1)
        brow = xd[0][0:1, :]
        yq = pair("yq", [T, D])
        xqi = pair("xqi", [T, D], I32, d=(1 if SIMFIX else 2))
        xqb = pair("xqb", [T, D], BF16)
        xqc = pair("xqc", [T, D], BF16)
        xqf = pair("xqf", [T, D], BF16, d=2)
        xn2 = pair("xn2", [T, D], BF16, d=2)
        xn2f = pair("xn2f", [T, D], BF16, d=2)
        if SIMFIX:
            tcl = pair("tcl", [T, D], BF16, d=2)
            sg = pair("sg", [T, D], BF16, d=2)
            eu = pair("eu", [T, D], BF16, d=2)
            ed = pair("ed", [T, D], BF16, d=2)
            cnh = sb("cnh", [T, 1], F32)
        else:
            dump_v = sb("dump_v", [T, D], BF16)
            dump_s = sb("dump_s", [T, 512])
        dsq = xd[0] if SIMFIX else dump_s
        dsqB = "xd0" if SIMFIX else "dump_s"
        brow2 = xc[0][0:1, :]
        ef = pair("ef", [T, 2 * PF], d=2)
        qpb = pair("qpb", [T, PF], BF16, d=2)
        kpr = sb("kpr", [T, PF * nt], BF16)
        qksb = pair("qksb", [T, 2 * PF], d=2)
        stat_names = ("mx", "mn", "s1", "s2", "m", "m2t", "var", "lnv", "rstd",
                      "ta", "tb", "tc", "sa", "sb", "amax", "inv", "a_s", "b_s", "xsh",
                      "zsum", "zse", "b_m")
        st = {n: pair(f"st_{n}", [T, 1]) for n in stat_names}

        pf = [psum(f"pf{i}", [T, 512], F32) for i in range(4)]
        ptrs = [psum(f"ptr{i}", [T, D], BF16) for i in range(2)]
        pqk = [psum(f"pqk{i}", [T, 512], F32) for i in range(2)]

        sems = {}
        sem_names = ["V", "S", "G", "T", "YB"]
        sem_names += [f"YO{i}" for i in range(PAR)]
        sem_names += [f"YL{i}" for i in range(max(PAR, 4))]
        sem_names += [f"YS{i}" for i in range(2)]
        sem_names += [f"YN{i}" for i in range(4)]
        sem_names += [f"YK{i}" for i in range(3)]
        sem_names += [f"YQ{i}" for i in range(2)]

        for e in sem_names:
            sems[e] = ctx.enter_context(nc.semaphore(f"sem_{e}"))

        b_pf = [B(f"pf{i}") for i in range(4)]
        b_pqk = [B(f"pqk{i}") for i in range(2)]
        b_ptrs = [B(f"ptr{i}") for i in range(2)]
        ptr_ctr = [0]

        def next_ptr():
            i = ptr_ctr[0] & 1
            ptr_ctr[0] += 1
            return ptrs[i], b_ptrs[i]

        INV_D = 1.0 / D

        def _free(ap):
            n = 1
            for d in ap.shape[1:]:
                n *= d
            return n

        def _is16(ap):
            return mybir.dt.size(ap.dtype) == 2

        def ts(eng, out, in0, s1, s2, op0, op1=None, accum=None,
               reads=(), writes=()):
            kw = {}
            if op1 is not None:
                kw["op1"] = op1
            if accum is not None:
                kw["accum_out"] = accum
            f = _free(out)
            c = 100 + (f * 1.15 if accum is not None else
                       f * (0.3 if _is16(out) else 0.6))
            return pl.op(eng, lambda e, o=out, i=in0, a=s1, b=s2, kw=kw,
                         op0=op0: e.tensor_scalar(o, i, a, b, op0=op0, **kw),
                         reads=reads, writes=writes, cost=int(c))

        def tt(eng, out, a, b_, op, reads=(), writes=()):
            f = _free(out)
            if eng == "G":
                c = 250 + f * 2.15
            else:
                c = 160 + f * (0.6 if (_is16(out) and _is16(a)) else 1.15)
            return pl.op(eng, lambda e, o=out, x=a, y=b_, op=op:
                         e.tensor_tensor(o, x, y, op),
                         reads=reads, writes=writes, cost=int(c))

        def stt(eng, out, in0, scal, in1, op0, op1, reads=(), writes=()):
            f = _free(out)
            return pl.op(eng, lambda e, o=out, i0=in0, s=scal, i1=in1,
                         op0=op0, op1=op1:
                         e.scalar_tensor_tensor(o, i0, s, i1, op0=op0, op1=op1),
                         reads=reads, writes=writes, cost=int(180 + f * 1.3))

        def act(out, in_, func, bias=0.0, scale=1.0, accum=None,
                reads=(), writes=()):
            kw = {}
            if accum is not None:
                kw["accum_out"] = accum
            f = _free(out)
            return pl.op("S", lambda e, o=out, i=in_, f=func, b=bias, s=scale,
                         kw=kw: e.activation(o, i, f, bias=b, scale=s, **kw),
                         reads=reads, writes=writes, cost=int(300 + f * 0.85))

        def scopy(out, in_, reads=(), writes=()):
            f = _free(out)
            return pl.op("S", lambda e, o=out, i=in_: e.copy(o, i),
                         reads=reads, writes=writes, cost=int(300 + f * 0.85))

        def vcopy(eng, out, in_, reads=(), writes=()):
            f = _free(out)
            c = 130 + f * (0.55 if _is16(out) else 0.7)
            if eng == "G":
                c = 250 + f * 1.1
            return pl.op(eng, lambda e, o=out, i=in_: e.tensor_copy(o, i),
                         reads=reads, writes=writes, cost=int(c))

        def vreduce(out, in_, op, reads=(), writes=()):
            f = _free(in_)
            return pl.op("V", lambda e, o=out, i=in_, p=op:
                         e.tensor_reduce(o, i, axis=mybir.AxisListType.X, op=p),
                         reads=reads, writes=writes, cost=int(180 + f * 0.75))

        def recip(out, in_, reads=(), writes=()):
            return pl.op("V", lambda e, o=out, i=in_: e.reciprocal(o, i),
                         reads=reads, writes=writes, cost=170)

        def mm(out, lhsT, rhs, start, stop, reads=(), writes=()):
            f = _free(out)
            return pl.op("T", lambda e, o=out, l=lhsT, r=rhs, a=start, b=stop:
                         e.matmul(o, l, r, start=a, stop=b),
                         reads=reads, writes=writes, cost=int(60 + f * 0.43))

        def tr(out, in_, reads=(), writes=()):
            return pl.op("T", lambda e, o=out, i=in_:
                         e.transpose(o, i, identb[:]),
                         reads=tuple(reads) + (B("identb"),), writes=writes,
                         cost=130)

        def trans_group(src_ap_fn, n, src_reads):
            pt, bpt = next_ptr()
            for k in range(n):
                tr(pt[:, T * k:T * (k + 1)], src_ap_fn(k),
                   reads=src_reads, writes=[bpt])
            return pt, bpt

        def gdma(out, in_, stream, accum=None, reads=(), writes=()):
            kw = {} if accum is None else {"accum_op": accum}
            f = _free(out)
            return pl.op("G", lambda e, o=out, i=in_, kw=kw:
                         e.dma_start(out=o, in_=i, **kw),
                         reads=reads, writes=writes, sem_key=stream,
                         cost=500, lat=500 + int(f * 3.4), dma=True)

        def dma(out, in_, stream, reads=(), writes=()):
            f = _free(out)
            return pl.op("Y", lambda e, o=out, i=in_: e.dma_start(out=o, in_=i),
                         reads=reads, writes=writes, sem_key=stream,
                         cost=250, lat=500 + int(f * 3.4))

        pl.op("G", lambda e: e.memset(identb[:], 0.0), writes=[B("identb")])
        pl.op("G", lambda e: e.affine_select(
            out=identb[:], in_=identb[:], compare_op=ALU.not_equal, fill=1.0,
            base=0, pattern=[[-1, T]], channel_multiplier=1),
            reads=[B("identb")], writes=[B("identb")])
        pl.op("G", lambda e: e.memset(ceps[:], EPS_LN), writes=[B("ceps")])
        if SIMFIX:
            pl.op("G", lambda e: e.memset(cnh[:], -0.5), writes=[B("cnh")])
        pl.op("G", lambda e: e.memset(onesf[:], 1.0), writes=[B("onesf")])

        def emit_quant(t, src_ap, src_buf, scale_ap, bias_ap,
                       biasm_ap=None, biasm_buf=None, cast_eng="V"):
            p = t % PAR
            pi = t % len(xqi)
            pl.op("S", lambda e, o=xqi[pi][:], i=src_ap, b=bias_ap, sc=scale_ap:
                  e.activation(o, i, AF.Identity, bias=b, scale=sc),
                  reads=[src_buf, B(f"st_a_s{p}"), B(f"st_b_s{p}")],
                  writes=[B(f"xqi{pi}")], cost=int(300 + D * 0.85))
            if not SIMFIX:
                if cast_eng == "S":
                    scopy(xqc[p][:], xqi[pi][:], reads=[B(f"xqi{pi}")],
                          writes=[B(f"xqc{p}")])
                else:
                    vcopy("V", xqc[p][:], xqi[pi][:], reads=[B(f"xqi{pi}")],
                          writes=[B(f"xqc{p}")])
                return
            p2 = t % 2
            pl.op("S", lambda e, o=tcl[p2][:], i=src_ap, b=biasm_ap,
                  sc=scale_ap:
                  e.activation(o, i, AF.Identity, bias=b, scale=sc),
                  reads=[src_buf, B(f"st_a_s{p}"), biasm_buf],
                  writes=[B(f"tcl{p2}")], cost=int(300 + D * 0.85))
            vcopy("V", xqb[p][:], xqi[pi][:], reads=[B(f"xqi{pi}")],
                  writes=[B(f"xqb{p}")])
            ts("V", sg[p2][:], xqb[p][:], -1.0, None, ALU.add,
               reads=[B(f"xqb{p}")], writes=[B(f"sg{p2}")])
            tt("V", eu[p2][:], tcl[p2][:], xqb[p][:], ALU.is_gt,
               reads=[B(f"tcl{p2}"), B(f"xqb{p}")], writes=[B(f"eu{p2}")])
            tt("V", ed[p2][:], tcl[p2][:], sg[p2][:], ALU.is_lt,
               reads=[B(f"tcl{p2}"), B(f"sg{p2}")], writes=[B(f"ed{p2}")])
            tt("V", xqc[p][:], xqb[p][:], eu[p2][:], ALU.add,
               reads=[B(f"xqb{p}"), B(f"eu{p2}")], writes=[B(f"xqc{p}")])
            tt("V", xqc[p][:], xqc[p][:], ed[p2][:], ALU.subtract,
               reads=[B(f"xqc{p}"), B(f"ed{p2}")], writes=[B(f"xqc{p}")])

        for oc in range(8):
            p = oc % PAR
            pb = oc % XBD
            dma(xb[pb][:], wf_d[T * oc:T * (oc + 1), :], f"YL{pb}",
                writes=[B(f"xb{pb}")])
            vreduce(st["mx"][p][:], xb[pb][:], ALU.max,
                    reads=[B(f"xb{pb}")], writes=[B(f"st_mx{p}")])
            vreduce(st["mn"][p][:], xb[pb][:], ALU.min,
                    reads=[B(f"xb{pb}")], writes=[B(f"st_mn{p}")])
            ts("V", st["tb"][p][:], st["mn"][p][:], -1.0, None, ALU.mult,
               reads=[B(f"st_mn{p}")], writes=[B(f"st_tb{p}")])
            tt("V", st["ta"][p][:], st["mx"][p][:], st["tb"][p][:], ALU.max,
               reads=[B(f"st_mx{p}"), B(f"st_tb{p}")], writes=[B(f"st_ta{p}")])
            ts("V", st["amax"][p][:], st["ta"][p][:], 1.0, 1e-8, ALU.mult,
               ALU.max, reads=[B(f"st_ta{p}")], writes=[B(f"st_amax{p}")])
            recip(st["inv"][p][:], st["amax"][p][:],
                  reads=[B(f"st_amax{p}")], writes=[B(f"st_inv{p}")])
            ts("V", st["a_s"][p][:], st["inv"][p][:], 127.0, None, ALU.mult,
               reads=[B(f"st_inv{p}")], writes=[B(f"st_a_s{p}")])
            pl.op("V", lambda e, o=st["b_s"][p][:]: e.memset(o, 0.0),
                  writes=[B(f"st_b_s{p}")])
            ts("V", st["xsh"][p][:], st["amax"][p][:], 1.0 / 127.0, None,
               ALU.mult, reads=[B(f"st_amax{p}")], writes=[B(f"st_xsh{p}")])
            emit_quant(oc, xb[pb][:], B(f"xb{pb}"), st["a_s"][p][:], 0.0,
                       biasm_ap=(cnh[:] if SIMFIX else None),
                       biasm_buf=(B("cnh") if SIMFIX else None))
            act(xqb[p][:], xqc[p][:], AF.Identity, scale=st["xsh"][p][:],
                reads=[B(f"xqc{p}"), B(f"st_xsh{p}")], writes=[B(f"xqb{p}")])
            pt, bpt = trans_group(
                lambda k: xqb[p][:, T * k:T * (k + 1)], 8, [B(f"xqb{p}")])
            dview = wqT[:].rearrange("p (dc c) -> p dc c", c=D)[:, :, T * oc:T * (oc + 1)]
            sview = pt[:].rearrange("p (dc c) -> p dc c", c=T)
            vcopy("V", dview, sview, reads=[bpt], writes=[B("wqT")])

        for ec in range(8):
            p = ec % XBD
            dma(xb[p][:, 0:PF], pj_d[T * ec:T * (ec + 1), :], f"YL{p}",
                writes=[B(f"xb{p}")])
            pl.op("S", lambda e, d=kvsb[:, PF * ec:PF * (ec + 1)],
                  s=xb[p][:, 0:PF]: e.copy(d, s),
                  reads=[B(f"xb{p}")], writes=[B("kvsb")])

        for (w_d, w_idx) in ((wq_d, 0), (wk_d, 1)):
            for ec in range(8):
                p = ec % XBD
                dma(xb[p][:], w_d[T * ec:T * (ec + 1), :], f"YL{p}",
                    writes=[B(f"xb{p}")])
                pl.op("S", lambda e, d=qptm[:, D * ec:D * (ec + 1)],
                      s=xb[p][:]: e.copy(d, s),
                      reads=[B(f"xb{p}")], writes=[B("qpf0")])
            for pb in range(2):
                for dh in range(2):
                    for ec in range(8):
                        mm(pf[2 * pb + dh][:],
                           kvsb[:, PF * ec + T * pb:PF * ec + T * (pb + 1)],
                           qptm[:, D * ec + 512 * dh:D * ec + 512 * (dh + 1)],
                           start=(ec == 0), stop=(ec == 7),
                           reads=[B("kvsb"), B("qpf0")],
                           writes=[b_pf[2 * pb + dh]])
            for pb in range(2):
                for dh in range(2):
                    vcopy("V", xqb[pb][:, 512 * dh:512 * (dh + 1)],
                          pf[2 * pb + dh][:],
                          reads=[b_pf[2 * pb + dh]], writes=[B(f"xqb{pb}")])
            for pb in range(2):
                pt, bpt = trans_group(
                    lambda k, pb=pb: xqb[pb][:, T * k:T * (k + 1)], 8,
                    [B(f"xqb{pb}")])
                dview = Wqkp[:].rearrange("p (dc c) -> p dc c", c=2 * PF)[
                    :, :, PF * w_idx + T * pb:PF * w_idx + T * (pb + 1)]
                sview = pt[:].rearrange("p (dc c) -> p dc c", c=T)
                vcopy("V", dview, sview, reads=[bpt], writes=[B("Wqkp")])

        for dc in range(8):
            p = dc % XBD
            dma(xb[p][:], wv_d[T * dc:T * (dc + 1), :], f"YL{p}",
                writes=[B(f"xb{p}")])
            pl.op("S", lambda e, d=qptm[:, D * dc:D * (dc + 1)],
                  s=xb[p][:]: e.copy(d, s),
                  reads=[B(f"xb{p}")], writes=[B("qpf0")])
        for ec in range(8):
            pt, bpt = trans_group(
                lambda k, ec=ec: qptm[:, D * k + T * ec:D * k + T * (ec + 1)],
                8, [B("qpf0")])
            vcopy("V", WvT[:, D * ec:D * (ec + 1)], pt[:],
                  reads=[bpt], writes=[B("WvT")])

        dma(brow, bi_d, "YB", writes=[B("brow")])
        ts("V", brow2, brow, 0.5, None, ALU.mult,
           reads=[B("brow")], writes=[B("xc0")])
        for nh in range(2):
            mm(pf[nh][:], onesf[:], brow2[:, 512 * nh:512 * (nh + 1)],
               start=True, stop=True,
               reads=[B("onesf"), B("xc0")], writes=[b_pf[nh]])
            vcopy("V", biash[:, 512 * nh:512 * (nh + 1)], pf[nh][:],
                  reads=[b_pf[nh]], writes=[B("biash")])

        def emit_stats_quant(p, src, src_buf, have_s1=False):
            vreduce(st["mx"][p][:], src, ALU.max,
                    reads=[src_buf], writes=[B(f"st_mx{p}")])
            vreduce(st["mn"][p][:], src, ALU.min,
                    reads=[src_buf], writes=[B(f"st_mn{p}")])
            if not have_s1:
                act(xd[0][:], src, AF.Identity, accum=st["s1"][p][:],
                    reads=[src_buf], writes=[B("xd0"), B(f"st_s1{p}")])
            act(dsq[:, 0:512], src[:, 0:512], AF.Square,
                accum=st["sa"][p][:],
                reads=[src_buf], writes=[B(dsqB), B(f"st_sa{p}")])
            act(dsq[:, 0:512], src[:, 512:1024], AF.Square,
                accum=st["sb"][p][:],
                reads=[src_buf], writes=[B(dsqB), B(f"st_sb{p}")])
            tt("V", st["s2"][p][:], st["sa"][p][:], st["sb"][p][:], ALU.add,
               reads=[B(f"st_sa{p}"), B(f"st_sb{p}")], writes=[B(f"st_s2{p}")])
            ts("V", st["m"][p][:], st["s1"][p][:], INV_D, None, ALU.mult,
               reads=[B(f"st_s1{p}")], writes=[B(f"st_m{p}")])
            tt("V", st["m2t"][p][:], st["m"][p][:], st["m"][p][:], ALU.mult,
               reads=[B(f"st_m{p}")], writes=[B(f"st_m2t{p}")])
            stt("V", st["var"][p][:], st["s2"][p][:], INV_D, st["m2t"][p][:],
                ALU.mult, ALU.subtract,
                reads=[B(f"st_s2{p}"), B(f"st_m2t{p}")], writes=[B(f"st_var{p}")])
            act(st["lnv"][p][:], st["var"][p][:], AF.Ln, bias=ceps[:],
                reads=[B(f"st_var{p}"), B("ceps")], writes=[B(f"st_lnv{p}")])
            act(st["rstd"][p][:], st["lnv"][p][:], AF.Exp, scale=-0.5,
                reads=[B(f"st_lnv{p}")], writes=[B(f"st_rstd{p}")])
            tt("V", st["ta"][p][:], st["mx"][p][:], st["m"][p][:], ALU.subtract,
               reads=[B(f"st_mx{p}"), B(f"st_m{p}")], writes=[B(f"st_ta{p}")])
            tt("V", st["tb"][p][:], st["m"][p][:], st["mn"][p][:], ALU.subtract,
               reads=[B(f"st_m{p}"), B(f"st_mn{p}")], writes=[B(f"st_tb{p}")])
            tt("V", st["tc"][p][:], st["ta"][p][:], st["tb"][p][:], ALU.max,
               reads=[B(f"st_ta{p}"), B(f"st_tb{p}")], writes=[B(f"st_tc{p}")])
            ts("V", st["amax"][p][:], st["tc"][p][:], st["rstd"][p][:], 1e-8,
               ALU.mult, ALU.max,
               reads=[B(f"st_tc{p}"), B(f"st_rstd{p}")], writes=[B(f"st_amax{p}")])
            recip(st["inv"][p][:], st["amax"][p][:],
                  reads=[B(f"st_amax{p}")], writes=[B(f"st_inv{p}")])
            ts("V", st["a_s"][p][:], st["inv"][p][:], 127.0, st["rstd"][p][:],
               ALU.mult, ALU.mult,
               reads=[B(f"st_inv{p}"), B(f"st_rstd{p}")], writes=[B(f"st_a_s{p}")])
            ts("V", st["b_s"][p][:], st["m"][p][:], st["a_s"][p][:], -1.0,
               ALU.mult, ALU.mult,
               reads=[B(f"st_m{p}"), B(f"st_a_s{p}")], writes=[B(f"st_b_s{p}")])
            ts("V", st["xsh"][p][:], st["amax"][p][:], 0.5 / 127.0, None,
               ALU.mult, reads=[B(f"st_amax{p}")], writes=[B(f"st_xsh{p}")])
            if SIMFIX:
                ts("V", st["b_m"][p][:], st["b_s"][p][:], -0.5, None, ALU.add,
                   reads=[B(f"st_b_s{p}")], writes=[B(f"st_b_m{p}")])

        def ffn_pf(t):
            return ((2 * t) % 4, (2 * t + 1) % 4)

        def emit_trans_ffn(t, evac_eng="S"):
            p = t % PAR
            p2 = t % 2
            pt, bpt = trans_group(
                lambda k: xqc[p][:, T * k:T * (k + 1)], 8, [B(f"xqc{p}")])
            if evac_eng == "S":
                scopy(xqf[p2][:], pt[:], reads=[bpt], writes=[B(f"xqf{p2}")])
            else:
                vcopy("V", xqf[p2][:], pt[:], reads=[bpt],
                      writes=[B(f"xqf{p2}")])
            j0, j1 = ffn_pf(t)
            for nh, j in ((0, j0), (1, j1)):
                for c in range(8):
                    mm(pf[j][:], xqf[p2][:, T * c:T * (c + 1)],
                       wqT[:, D * c + 512 * nh:D * c + 512 * (nh + 1)],
                       start=(c == 0), stop=(c == 7),
                       reads=[B(f"xqf{p2}"), B("wqT")], writes=[b_pf[j]])

        def emit_epilogue(t, p, src_biased, src_biased_buf, dst, dst_buf):
            acc = (st["ta"][p], st["tb"][p])
            accb = (B(f"st_ta{p}"), B(f"st_tb{p}"))
            j0, j1 = ffn_pf(t)
            for nh, j in ((0, j0), (1, j1)):
                pl.op("V", lambda e, o=dst[:, 512 * nh:512 * (nh + 1)],
                      i0=pf[j][:], sc=st["xsh"][p][:],
                      i1=src_biased[:, 512 * nh:512 * (nh + 1)],
                      a=acc[nh][:]:
                      e.scalar_tensor_tensor(o, i0, sc, i1, op0=ALU.mult,
                                             op1=ALU.add, accum_out=a),
                      reads=[b_pf[j], B(f"st_xsh{p}"), src_biased_buf],
                      writes=[dst_buf, accb[nh]], cost=int(180 + 512 * 1.3))
            tt("V", st["s1"][p][:], st["ta"][p][:], st["tb"][p][:], ALU.add,
               reads=[accb[0], accb[1]], writes=[B(f"st_s1{p}")])

        def emit_load(t, dram, tag):
            px = t % XBD
            reads = [B(f"spill{t % 2}")] if tag == "s" else []
            dma(xb[px][:], dram[T * t:T * (t + 1), :], f"YL{px}",
                reads=reads, writes=[B(f"xb{px}")])

        def stage_a1(t):
            p = t % PAR
            px = t % XBD
            emit_stats_quant(p, xb[px][:], B(f"xb{px}"))
            emit_quant(t, xb[px][:], B(f"xb{px}"), st["a_s"][p][:],
                       st["b_s"][p][:], biasm_ap=st["b_m"][p][:],
                       biasm_buf=B(f"st_b_m{p}"))

        def stage_b1(t):
            p = t % PAR
            px = t % XBD
            emit_trans_ffn(t, evac_eng="V")
            tt("V", xc[p][:], xb[px][:], biash[:], ALU.add,
               reads=[B(f"xb{px}"), B("biash")], writes=[B(f"xc{p}")])
            emit_epilogue(t, p, xc[p][:], B(f"xc{p}"), xb[px][:], B(f"xb{px}"))
            dma(spill_d[T * t:T * (t + 1), :], xb[px][:], f"YS{t % 2}",
                reads=[B(f"xb{px}")], writes=[B(f"spill{t % 2}")])

        def stage_c1(t):
            p = t % PAR
            p2 = t % 2
            px = t % XBD
            act(dsq[:, 0:512], xb[px][:, 0:512], AF.Square,
                accum=st["sa"][p][:],
                reads=[B(f"xb{px}")], writes=[B(dsqB), B(f"st_sa{p}")])
            act(dsq[:, 0:512], xb[px][:, 512:1024], AF.Square,
                accum=st["sb"][p][:],
                reads=[B(f"xb{px}")], writes=[B(dsqB), B(f"st_sb{p}")])
            tt("V", st["s2"][p][:], st["sa"][p][:], st["sb"][p][:], ALU.add,
               reads=[B(f"st_sa{p}"), B(f"st_sb{p}")], writes=[B(f"st_s2{p}")])
            ts("V", st["m"][p][:], st["s1"][p][:], INV_D, None, ALU.mult,
               reads=[B(f"st_s1{p}")], writes=[B(f"st_m{p}")])
            tt("V", st["m2t"][p][:], st["m"][p][:], st["m"][p][:], ALU.mult,
               reads=[B(f"st_m{p}")], writes=[B(f"st_m2t{p}")])
            stt("V", st["var"][p][:], st["s2"][p][:], INV_D, st["m2t"][p][:],
                ALU.mult, ALU.subtract,
                reads=[B(f"st_s2{p}"), B(f"st_m2t{p}")],
                writes=[B(f"st_var{p}")])
            act(st["lnv"][p][:], st["var"][p][:], AF.Ln, bias=ceps[:],
                reads=[B(f"st_var{p}"), B("ceps")], writes=[B(f"st_lnv{p}")])
            act(st["rstd"][p][:], st["lnv"][p][:], AF.Exp, scale=-0.5,
                reads=[B(f"st_lnv{p}")], writes=[B(f"st_rstd{p}")])
            ts("V", st["b_s"][p][:], st["m"][p][:], st["rstd"][p][:], -1.0,
               ALU.mult, ALU.mult,
               reads=[B(f"st_m{p}"), B(f"st_rstd{p}")], writes=[B(f"st_b_s{p}")])
            act(xn2[p2][:], xb[px][:], AF.Identity, bias=st["b_s"][p][:],
                scale=st["rstd"][p][:],
                reads=[B(f"xb{px}"), B(f"st_b_s{p}"), B(f"st_rstd{p}")],
                writes=[B(f"xn2{p2}")])
            pt, bpt = trans_group(
                lambda k: xn2[p2][:, T * k:T * (k + 1)], 8, [B(f"xn2{p2}")])
            scopy(xn2f[p2][:], pt[:], reads=[bpt], writes=[B(f"xn2f{p2}")])
            jq = t % 2
            for c in range(8):
                mm(pqk[jq][:, 0:2 * PF], xn2f[p2][:, T * c:T * (c + 1)],
                   Wqkp[:, 2 * PF * c:2 * PF * (c + 1)],
                   start=(c == 0), stop=(c == 7),
                   reads=[B(f"xn2f{p2}"), B("Wqkp")], writes=[b_pqk[jq]])
            scopy(qksb[p2][:], pqk[jq][:, 0:2 * PF],
                  reads=[b_pqk[jq]], writes=[B(f"qksb{p2}")])
            act(ef[p2][:], qksb[p2][:, 0:2 * PF], AF.Exp,
                reads=[B(f"qksb{p2}")], writes=[B(f"ef{p2}")])
            ts("V", ef[p2][:], ef[p2][:], 1.0, None, ALU.min,
               reads=[B(f"ef{p2}")], writes=[B(f"ef{p2}")])
            pl.op("V", lambda e, o=qpb[p2][:], i0=qksb[p2][:, 0:PF],
                  i1=ef[p2][:, 0:PF], a=st["zsum"][p][:]:
                  e.scalar_tensor_tensor(o, i0, 0.0, i1, op0=ALU.max,
                                         op1=ALU.add, accum_out=a),
                  reads=[B(f"qksb{p2}"), B(f"ef{p2}")],
                  writes=[B(f"qpb{p2}"), B(f"st_zsum{p}")],
                  cost=int(180 + PF * 1.3))
            stt("V", kpr[:, PF * t:PF * (t + 1)], qksb[p2][:, PF:2 * PF],
                0.0, ef[p2][:, PF:2 * PF], ALU.max, ALU.add,
                reads=[B(f"qksb{p2}"), B(f"ef{p2}")],
                writes=[B("kpr")])
            ts("V", st["zse"][p][:], st["zsum"][p][:], EPS_Z, None, ALU.add,
               reads=[B(f"st_zsum{p}")], writes=[B(f"st_zse{p}")])
            recip(zst[:, t:t + 1], st["zse"][p][:],
                  reads=[B(f"st_zse{p}")], writes=[B("zst")])
            dma(qptm[:, PF * t:PF * (t + 1)], qpb[p2][:], f"YQ{p2}",
                reads=[B(f"qpb{p2}")], writes=[B("qpf0")])
            dma(xn2sp_d[T * t:T * (t + 1), :], xn2[p2][:], f"YN{t % 2}",
                reads=[B(f"xn2{p2}")], writes=[B(f"xn2sp_t{t}")])

        emit_load(0, x_d, "x")
        for t in range(nt):
            stage_a1(t)
            if t > 0:
                stage_c1(t - 1)
            if t + 1 < nt:
                emit_load(t + 1, x_d, "x")
            stage_b1(t)
        stage_c1(nt - 1)

        if SIMFIX:
            rl_bufs = [xn2[0], xn2[1], xn2f[0], xn2f[1]]
            rl_names = ["xn20", "xn21", "xn2f0", "xn2f1"]
            rl_streams = ["YN0", "YN1", "YN2", "YN3"]
        else:
            rl_bufs = [xn2[0], xn2[1], xn2f[0], xn2f[1], xqf[0], xqf[1],
                       dump_v]
            rl_names = ["xn20", "xn21", "xn2f0", "xn2f1", "xqf0", "xqf1",
                        "dump_v"]
            rl_streams = ["YN0", "YN1", "YN2", "YN3", "YK0", "YK1", "YK2"]
        nrl = len(rl_bufs)
        for t in range(nt):
            rb = t % nrl
            dma(rl_bufs[rb][:], xn2sp_d[T * t:T * (t + 1), :], rl_streams[rb],
                reads=[B(f"xn2sp_t{t}")], writes=[B(rl_names[rb])])
            for pc in range(2):
                for nh in range(2):
                    mm(pf[2 * pc + nh][:],
                       kpr[:, PF * t + T * pc:PF * t + T * (pc + 1)],
                       rl_bufs[rb][:, 512 * nh:512 * (nh + 1)],
                       start=(t == 0), stop=(t == nt - 1),
                       reads=[B("kpr"), B(rl_names[rb])],
                       writes=[b_pf[2 * pc + nh]])

        for pc in range(2):
            for nh in range(2):
                vcopy("V", xqf[pc][:, 512 * nh:512 * (nh + 1)],
                      pf[2 * pc + nh][:],
                      reads=[b_pf[2 * pc + nh]], writes=[B(f"xqf{pc}")])
        for pc in range(2):
            pt, bpt = trans_group(
                lambda k, pc=pc: xqf[pc][:, T * k:T * (k + 1)], 8,
                [B(f"xqf{pc}")])
            for j in range(2):
                dview = xn2[j][:].rearrange("p (k c) -> p k c", c=PF)[:, :, T * pc:T * (pc + 1)]
                sview = pt[:, 512 * j:512 * (j + 1)].rearrange("p (k c) -> p k c", c=T)
                vcopy("V", dview, sview, reads=[bpt], writes=[B(f"xn2{j}")])
        for pc2 in range(2):
            for dh in range(2):
                for ec in range(8):
                    lhs = xn2[ec // 4][:, PF * (ec % 4) + T * pc2:
                                       PF * (ec % 4) + T * (pc2 + 1)]
                    mm(pqk[dh][:], lhs,
                       WvT[:, D * ec + 512 * dh:D * ec + 512 * (dh + 1)],
                       start=(ec == 0), stop=(ec == 7),
                       reads=[B(f"xn2{ec // 4}"), B("WvT")], writes=[b_pqk[dh]])
                vcopy("V", kvsb[:, D * pc2 + 512 * dh:D * pc2 + 512 * (dh + 1)],
                      pqk[dh][:], reads=[b_pqk[dh]], writes=[B("kvsb")])

        store_ops = []

        def emit_qp_trans(t):
            r = t % 4
            base = qpfr_off + 2 * T * r
            pt, bpt = trans_group(
                lambda k: qptm[:, PF * t + T * k:PF * t + T * (k + 1)], 2,
                [B("qpf0")])
            scopy(qptm[:, base:base + 2 * T], pt[:, 0:2 * T],
                  reads=[bpt], writes=[B(f"qpfr{r}")])

        def stage_n2(t):
            p = t % PAR
            px = t % XBD
            r = t % 4
            base = qpfr_off + 2 * T * r
            for nh in range(2):
                for pc in range(2):
                    mm(pqk[nh][:], qptm[:, base + T * pc:base + T * (pc + 1)],
                       kvsb[:, D * pc + 512 * nh:D * pc + 512 * (nh + 1)],
                       start=(pc == 0), stop=(pc == 1),
                       reads=[B(f"qpfr{r}"), B("kvsb")], writes=[b_pqk[nh]])
            acc = (st["ta"][p], st["tb"][p])
            accb = (B(f"st_ta{p}"), B(f"st_tb{p}"))
            for nh in range(2):
                pl.op("V", lambda e, o=xc[p][:, 512 * nh:512 * (nh + 1)],
                      i0=pqk[nh][:], sc=zst[:, t:t + 1],
                      i1=xb[px][:, 512 * nh:512 * (nh + 1)], a=acc[nh][:]:
                      e.scalar_tensor_tensor(o, i0, sc, i1, op0=ALU.mult,
                                             op1=ALU.add, accum_out=a),
                      reads=[b_pqk[nh], B("zst"), B(f"xb{px}")],
                      writes=[B(f"xc{p}"), accb[nh]], cost=int(180 + 512 * 1.3))
            tt("V", st["s1"][p][:], st["ta"][p][:], st["tb"][p][:], ALU.add,
               reads=[accb[0], accb[1]], writes=[B(f"st_s1{p}")])

        def stage_a2(t):
            p = t % PAR
            emit_stats_quant(p, xc[p][:], B(f"xc{p}"), have_s1=True)
            emit_quant(t, xc[p][:], B(f"xc{p}"), st["a_s"][p][:],
                       st["b_s"][p][:], biasm_ap=st["b_m"][p][:],
                       biasm_buf=B(f"st_b_m{p}"), cast_eng="S")

        def stage_b2(t):
            p = t % PAR
            p2 = t % 2
            emit_trans_ffn(t, evac_eng="S")
            tt("V", xd[0][:], xc[p][:], biash[:], ALU.add,
               reads=[B(f"xc{p}"), B("biash")], writes=[B("xd0")])
            emit_epilogue(t, p, xd[0][:], B("xd0"), xc[p][:], B(f"xc{p}"))
            store_ops.append(
                dma(out_d[T * t:T * (t + 1), :], xc[p][:], f"YO{p}",
                    reads=[B(f"xc{p}")]))

        emit_load(0, spill_d, "s")
        for t in range(min(4, nt)):
            emit_qp_trans(t)
        for t in range(nt):
            stage_n2(t)
            if t + 4 < nt:
                emit_qp_trans(t + 4)
            stage_a2(t)
            if t + 1 < nt:
                emit_load(t + 1, spill_d, "s")
            if t > 0:
                stage_b2(t - 1)
        stage_b2(nt - 1)

        pl.op("Y", lambda e: e.nop(), extra=store_ops, sem_key="YO0")

        import os as _os
        if _os.environ.get("KSCHED", "1") == "1":
            pl.schedule()
        pl.finalize()
        with nc.Block() as block:
            @block.vector
            def _(eng):
                pl.emit_engine("V", eng, sems)

            @block.scalar
            def _(eng):
                pl.emit_engine("S", eng, sems)

            @block.gpsimd
            def _(eng):
                pl.emit_engine("G", eng, sems)

            @block.tensor
            def _(eng):
                pl.emit_engine("T", eng, sems)

            @block.sync
            def _(eng):
                pl.emit_engine("Y", eng, sems)

    return nc


def kernel(L0_attn_x, L0_attn_Wq, L0_attn_Wk, L0_attn_Wv, L0_attn_proj,
           L0_ffn_weight, L0_ffn_bias):
    x = np.ascontiguousarray(np.asarray(L0_attn_x, dtype=np.float32))
    wq = np.ascontiguousarray(np.asarray(L0_attn_Wq, dtype=np.float32))
    wk = np.ascontiguousarray(np.asarray(L0_attn_Wk, dtype=np.float32))
    wv = np.ascontiguousarray(np.asarray(L0_attn_Wv, dtype=np.float32))
    pj = np.ascontiguousarray(np.asarray(L0_attn_proj, dtype=np.float32))
    wf = np.ascontiguousarray(np.asarray(L0_ffn_weight, dtype=np.float32))
    bi = np.ascontiguousarray(
        np.asarray(L0_ffn_bias, dtype=np.float32).reshape(1, D))
    nc = build_nc()
    in_maps = [{"x": x[b], "wq": wq, "wk": wk, "wv": wv, "proj": pj,
                "wf": wf, "bias": bi} for b in range(NCORES)]
    res = run_bass_kernel_spmd(nc, in_maps, core_ids=list(range(NCORES)))
    return np.stack([np.asarray(r["out"], dtype=np.float32)
                     for r in res.results], axis=0)


# version 40
# speedup vs baseline: 1.0281x; 1.0281x over previous
import os
import numpy as np
import concourse.bass as bass
import concourse.mybir as mybir
from concourse.bass_utils import run_bass_kernel_spmd
from contextlib import ExitStack

F32 = mybir.dt.float32
BF16 = mybir.dt.bfloat16
I32 = mybir.dt.int32
I16 = mybir.dt.int16
AF = mybir.ActivationFunctionType
ALU = mybir.AluOpType

NCORES = 8
SEQ = 8192
D = 1024
PF = 256
T = 128
NT = SEQ // T
EPS_LN = 1e-5
EPS_Z = 1e-8
PAR = 3
PARANOID = True

ENGS = ("V", "S", "G", "T", "Y")


class _Op:
    __slots__ = ("engine", "emit", "waits", "semval", "is_dma", "sem_key",
                 "idx", "cost", "lat")

    def __init__(self, engine, emit, waits, is_dma, sem_key, idx, cost, lat):
        self.engine = engine
        self.emit = emit
        self.waits = waits
        self.semval = None
        self.is_dma = is_dma
        self.sem_key = sem_key
        self.idx = idx
        self.cost = cost
        self.lat = lat


class _Buf:
    __slots__ = ("name", "last_write", "readers")

    def __init__(self, name):
        self.name = name
        self.last_write = None
        self.readers = []


class Plan:
    def __init__(self):
        self.ops = {e: [] for e in ENGS}
        self.bufs = {}
        self.nops = 0

    def buf(self, name):
        b = self.bufs.get(name)
        if b is None:
            b = _Buf(name)
            self.bufs[name] = b
        return b

    def op(self, engine, emit, reads=(), writes=(), extra=(), sem_key=None,
           cost=None, lat=None, dma=False):
        waits = set()
        for b in reads:
            if b.last_write is not None:
                waits.add(b.last_write)
        for b in writes:
            waits.update(b.readers)
            if b.last_write is not None:
                waits.add(b.last_write)
        for o in extra:
            if o is not None:
                waits.add(o)
        is_dma = dma or engine == "Y"
        if cost is None:
            cost = 300 if is_dma else 400
        if lat is None:
            lat = 2500 if is_dma else 200
        o = _Op(engine, emit, waits, is_dma, sem_key or engine,
                self.nops, cost, lat)
        self.nops += 1
        self.ops[engine].append(o)
        for b in reads:
            b.readers.append(o)
        for b in writes:
            b.last_write = o
            b.readers = []
        return o

    def schedule(self):
        import heapq
        allops = sorted((o for el in self.ops.values() for o in el),
                        key=lambda o: o.idx)
        succs = {o: [] for o in allops}
        indeg = {o: 0 for o in allops}
        for o in allops:
            for w in o.waits:
                succs[w].append(o)
                indeg[o] += 1
        pr = {}
        for o in reversed(allops):
            m = 0
            for t in succs[o]:
                if pr[t] > m:
                    m = pr[t]
            pr[o] = m + o.cost + o.lat
        engine_free = {e: 0 for e in self.ops}
        ready = {e: [] for e in self.ops}
        completions = []
        order = {e: [] for e in self.ops}
        import os as _os
        sched_engs = set(_os.environ.get("KENG", "VSGTY"))

        def _key(o):
            if o.engine[0] not in sched_engs:
                return (o.idx, o.idx, o)
            return (-pr[o], o.idx, o)
        for o in allops:
            if indeg[o] == 0:
                heapq.heappush(ready[o.engine], _key(o))
        ndone = 0
        now = 0
        while ndone < len(allops):
            progressed = True
            while progressed:
                progressed = False
                for e in self.ops:
                    if engine_free[e] <= now and ready[e]:
                        _, _, o = heapq.heappop(ready[e])
                        start = now
                        engine_free[e] = start + o.cost
                        heapq.heappush(completions,
                                       (start + o.cost + o.lat, o.idx, o))
                        order[e].append(o)
                        ndone += 1
                        progressed = True
            cand = []
            if completions:
                cand.append(completions[0][0])
            for e in self.ops:
                if ready[e] and engine_free[e] > now:
                    cand.append(engine_free[e])
            if not cand:
                if not completions:
                    break
                cand.append(completions[0][0])
            now = min(cand)
            while completions and completions[0][0] <= now:
                _, _, o = heapq.heappop(completions)
                for t in succs[o]:
                    indeg[t] -= 1
                    if indeg[t] == 0:
                        heapq.heappush(ready[t.engine], _key(t))
        assert ndone == len(allops), f"scheduler stuck: {ndone}/{len(allops)}"
        self.ops = order
        return now

    def _needs_sem(self, producer, consumer_engine):
        if producer.engine == "T" and consumer_engine == "T":
            return False
        return (producer.engine != consumer_engine or producer.is_dma
                or PARANOID)

    def transitive_reduce(self):
        spos = {}
        for e, el in self.ops.items():
            for k, o in enumerate(el):
                spos[o] = k
        order = []
        for el in self.ops.values():
            order.extend(el)
        indeg = {o: 0 for o in order}
        succs = {o: [] for o in order}
        for o in order:
            for w in o.waits:
                succs[w].append(o)
                indeg[o] += 1
        from collections import deque
        q = deque(o for o in order if indeg[o] == 0)
        wm = {}
        topo = []
        while q:
            o = q.popleft()
            topo.append(o)
            for t in succs[o]:
                indeg[t] -= 1
                if indeg[t] == 0:
                    q.append(t)
        assert len(topo) == len(order), "reduce topo stuck"
        for o in topo:
            m = {}
            kept = []
            for w in sorted(o.waits, key=lambda w: -spos[w]):
                lane = (w.engine, w.sem_key)
                if m.get(lane, -1) >= spos[w]:
                    continue
                kept.append(w)
                wmw = wm.get(w)
                if wmw:
                    for k2, v2 in wmw.items():
                        if m.get(k2, -1) < v2:
                            m[k2] = v2
                if m.get(lane, -1) < spos[w]:
                    m[lane] = spos[w]
            o.waits = set(kept)
            mo = dict(m)
            mo[(o.engine, o.sem_key)] = spos[o]
            wm[o] = mo

    def finalize(self):
        self.transitive_reduce()
        needed = set()
        for elist in self.ops.values():
            for o in elist:
                for w in o.waits:
                    if self._needs_sem(w, o.engine):
                        needed.add(w)
        ctr = {}
        for elist in self.ops.values():
            for o in elist:
                if o in needed or o.is_dma:
                    v = ctr.get(o.sem_key, 0) + (16 if o.is_dma else 1)
                    ctr[o.sem_key] = v
                    o.semval = v

    def emit_engine(self, ename, eng, sems):
        waited = {}
        for o in self.ops[ename]:
            by = {}
            for w in o.waits:
                if w.semval is None or not self._needs_sem(w, ename):
                    continue
                if by.get(w.sem_key, 0) < w.semval:
                    by[w.sem_key] = w.semval
            for k2 in sorted(by):
                v = by[k2]
                if waited.get(k2, 0) >= v:
                    continue
                eng.wait_ge(sems[k2], v)
                waited[k2] = v
            inst = o.emit(eng)
            if o.semval is not None:
                inst.then_inc(sems[o.sem_key], 16 if o.is_dma else 1)


def build_nc(nt=NT):
    nc = bass.Bass("TRN2", target_bir_lowering=False, debug=False)
    seq = nt * T
    x_d = nc.dram_tensor("x", [seq, D], F32, kind="ExternalInput").ap()
    wq_d = nc.dram_tensor("wq", [D, D], F32, kind="ExternalInput").ap()
    wk_d = nc.dram_tensor("wk", [D, D], F32, kind="ExternalInput").ap()
    wv_d = nc.dram_tensor("wv", [D, D], F32, kind="ExternalInput").ap()
    pj_d = nc.dram_tensor("proj", [D, PF], F32, kind="ExternalInput").ap()
    wf_d = nc.dram_tensor("wf", [D, D], F32, kind="ExternalInput").ap()
    bi_d = nc.dram_tensor("bias", [1, D], F32, kind="ExternalInput").ap()
    out_d = nc.dram_tensor("out", [seq, D], F32, kind="ExternalOutput").ap()
    spill_d = nc.dram_tensor("spill", [seq, D], F32, kind="Internal").ap()
    xn2sp_d = nc.dram_tensor("xn2sp", [seq, D], BF16, kind="Internal").ap()

    pl = Plan()
    B = pl.buf

    with ExitStack() as ctx:
        def sb(name, shape, dt=F32):
            return ctx.enter_context(nc.sbuf_tensor(name, shape, dt))

        def psum(name, shape, dt=F32):
            return ctx.enter_context(nc.psum_tensor(name, shape, dt))

        wqT = sb("wqT", [T, 8 * D], BF16)
        Wqkp = sb("Wqkp", [T, 8 * 2 * PF], BF16)
        WvT = sb("WvT", [T, 8 * D], BF16)
        biash = sb("biash", [T, D], F32)
        qptm = sb("qptm", [T, max(PF * nt, 8 * D) + 4 * 2 * T], BF16)
        qpfr_off = max(PF * nt, 8 * D)
        zst = sb("zst", [T, nt], F32)
        kvsb = sb("kvsb", [T, 2 * D], BF16)
        identb = sb("identb", [T, T], BF16)
        onesf = sb("onesf", [1, T], F32)
        ceps = sb("ceps", [T, 1], F32)

        def pair(name, shape, dt=F32, d=None):
            return [sb(f"{name}{i}", shape, dt) for i in range(d or PAR)]

        SIMFIX = os.environ.get("BASS_SIM_FP64", "0") == "1"
        XBD = 2 if SIMFIX else max(PAR, 4)
        xb = pair("xb", [T, D], d=XBD)
        xc = pair("xc", [T, D])
        xd = pair("xd", [T, D], d=1)
        brow = xd[0][0:1, :]
        yq = pair("yq", [T, D])
        xqi = pair("xqi", [T, D], I32, d=(1 if SIMFIX else 2))
        xqb = pair("xqb", [T, D], BF16)
        xqc = pair("xqc", [T, D], BF16)
        xqf = pair("xqf", [T, D], BF16, d=2)
        xn2 = pair("xn2", [T, D], BF16, d=2)
        xn2f = pair("xn2f", [T, D], BF16, d=2)
        if SIMFIX:
            tcl = pair("tcl", [T, D], BF16, d=2)
            sg = pair("sg", [T, D], BF16, d=2)
            eu = pair("eu", [T, D], BF16, d=2)
            ed = pair("ed", [T, D], BF16, d=2)
            cnh = sb("cnh", [T, 1], F32)
        else:
            dump_v = sb("dump_v", [T, D], BF16)
            dump_s = sb("dump_s", [T, 512])
        dsq = xd[0] if SIMFIX else dump_s
        dsqB = "xd0" if SIMFIX else "dump_s"
        brow2 = xc[0][0:1, :]
        ef = pair("ef", [T, 2 * PF], d=2)
        qpb = pair("qpb", [T, PF], BF16, d=2)
        kpr = sb("kpr", [T, PF * nt], BF16)
        qksb = pair("qksb", [T, 2 * PF], d=2)
        stat_names = ("mx", "mn", "s1", "s2", "m", "m2t", "var", "lnv", "rstd",
                      "ta", "tb", "tc", "sa", "sb", "amax", "inv", "a_s", "b_s", "xsh",
                      "zsum", "zse", "b_m")
        st = {n: pair(f"st_{n}", [T, 1]) for n in stat_names}

        pf = [psum(f"pf{i}", [T, 512], F32) for i in range(4)]
        ptrs = [psum(f"ptr{i}", [T, D], BF16) for i in range(2)]
        pqk = [psum(f"pqk{i}", [T, 512], F32) for i in range(2)]

        sems = {}
        sem_names = ["V", "S", "G", "T", "YB"]
        sem_names += [f"YO{i}" for i in range(PAR)]
        sem_names += [f"YL{i}" for i in range(max(PAR, 4))]
        sem_names += [f"YS{i}" for i in range(2)]
        sem_names += [f"YN{i}" for i in range(4)]
        sem_names += [f"YK{i}" for i in range(3)]
        sem_names += [f"YQ{i}" for i in range(2)]

        for e in sem_names:
            sems[e] = ctx.enter_context(nc.semaphore(f"sem_{e}"))

        b_pf = [B(f"pf{i}") for i in range(4)]
        b_pqk = [B(f"pqk{i}") for i in range(2)]
        b_ptrs = [B(f"ptr{i}") for i in range(2)]
        ptr_ctr = [0]

        def next_ptr():
            i = ptr_ctr[0] & 1
            ptr_ctr[0] += 1
            return ptrs[i], b_ptrs[i]

        INV_D = 1.0 / D

        def _free(ap):
            n = 1
            for d in ap.shape[1:]:
                n *= d
            return n

        def _is16(ap):
            return mybir.dt.size(ap.dtype) == 2

        def ts(eng, out, in0, s1, s2, op0, op1=None, accum=None,
               reads=(), writes=()):
            kw = {}
            if op1 is not None:
                kw["op1"] = op1
            if accum is not None:
                kw["accum_out"] = accum
            f = _free(out)
            c = 100 + (f * 1.15 if accum is not None else
                       f * (0.3 if _is16(out) else 0.6))
            return pl.op(eng, lambda e, o=out, i=in0, a=s1, b=s2, kw=kw,
                         op0=op0: e.tensor_scalar(o, i, a, b, op0=op0, **kw),
                         reads=reads, writes=writes, cost=int(c))

        def tt(eng, out, a, b_, op, reads=(), writes=()):
            f = _free(out)
            if eng == "G":
                c = 250 + f * 2.15
            else:
                c = 160 + f * (0.6 if (_is16(out) and _is16(a)) else 1.15)
            return pl.op(eng, lambda e, o=out, x=a, y=b_, op=op:
                         e.tensor_tensor(o, x, y, op),
                         reads=reads, writes=writes, cost=int(c))

        def stt(eng, out, in0, scal, in1, op0, op1, reads=(), writes=()):
            f = _free(out)
            return pl.op(eng, lambda e, o=out, i0=in0, s=scal, i1=in1,
                         op0=op0, op1=op1:
                         e.scalar_tensor_tensor(o, i0, s, i1, op0=op0, op1=op1),
                         reads=reads, writes=writes, cost=int(180 + f * 1.3))

        def act(out, in_, func, bias=0.0, scale=1.0, accum=None,
                reads=(), writes=()):
            kw = {}
            if accum is not None:
                kw["accum_out"] = accum
            f = _free(out)
            return pl.op("S", lambda e, o=out, i=in_, f=func, b=bias, s=scale,
                         kw=kw: e.activation(o, i, f, bias=b, scale=s, **kw),
                         reads=reads, writes=writes, cost=int(300 + f * 0.85))

        def scopy(out, in_, reads=(), writes=()):
            f = _free(out)
            return pl.op("S", lambda e, o=out, i=in_: e.copy(o, i),
                         reads=reads, writes=writes, cost=int(300 + f * 0.85))

        def vcopy(eng, out, in_, reads=(), writes=()):
            f = _free(out)
            c = 130 + f * (0.55 if _is16(out) else 0.7)
            if eng == "G":
                c = 250 + f * 1.1
            return pl.op(eng, lambda e, o=out, i=in_: e.tensor_copy(o, i),
                         reads=reads, writes=writes, cost=int(c))

        def vreduce(out, in_, op, reads=(), writes=()):
            f = _free(in_)
            return pl.op("V", lambda e, o=out, i=in_, p=op:
                         e.tensor_reduce(o, i, axis=mybir.AxisListType.X, op=p),
                         reads=reads, writes=writes, cost=int(180 + f * 0.75))

        def recip(out, in_, reads=(), writes=()):
            return pl.op("V", lambda e, o=out, i=in_: e.reciprocal(o, i),
                         reads=reads, writes=writes, cost=170)

        def mm(out, lhsT, rhs, start, stop, reads=(), writes=()):
            f = _free(out)
            return pl.op("T", lambda e, o=out, l=lhsT, r=rhs, a=start, b=stop:
                         e.matmul(o, l, r, start=a, stop=b),
                         reads=reads, writes=writes, cost=int(60 + f * 0.43))

        def tr(out, in_, reads=(), writes=()):
            return pl.op("T", lambda e, o=out, i=in_:
                         e.transpose(o, i, identb[:]),
                         reads=tuple(reads) + (B("identb"),), writes=writes,
                         cost=130)

        def trans_group(src_ap_fn, n, src_reads):
            pt, bpt = next_ptr()
            for k in range(n):
                tr(pt[:, T * k:T * (k + 1)], src_ap_fn(k),
                   reads=src_reads, writes=[bpt])
            return pt, bpt

        def gdma(out, in_, stream, accum=None, reads=(), writes=()):
            kw = {} if accum is None else {"accum_op": accum}
            f = _free(out)
            return pl.op("G", lambda e, o=out, i=in_, kw=kw:
                         e.dma_start(out=o, in_=i, **kw),
                         reads=reads, writes=writes, sem_key=stream,
                         cost=500, lat=500 + int(f * 3.4), dma=True)

        def dma(out, in_, stream, reads=(), writes=()):
            f = _free(out)
            return pl.op("Y", lambda e, o=out, i=in_: e.dma_start(out=o, in_=i),
                         reads=reads, writes=writes, sem_key=stream,
                         cost=250, lat=500 + int(f * 3.4))

        pl.op("G", lambda e: e.memset(identb[:], 0.0), writes=[B("identb")])
        pl.op("G", lambda e: e.affine_select(
            out=identb[:], in_=identb[:], compare_op=ALU.not_equal, fill=1.0,
            base=0, pattern=[[-1, T]], channel_multiplier=1),
            reads=[B("identb")], writes=[B("identb")])
        pl.op("G", lambda e: e.memset(ceps[:], EPS_LN), writes=[B("ceps")])
        if SIMFIX:
            pl.op("G", lambda e: e.memset(cnh[:], -0.5), writes=[B("cnh")])
        pl.op("G", lambda e: e.memset(onesf[:], 1.0), writes=[B("onesf")])

        for ec in range(8):
            p = ec % XBD
            dma(xb[p][:, 0:PF], pj_d[T * ec:T * (ec + 1), :], f"YL{p}",
                writes=[B(f"xb{p}")])
            pl.op("S", lambda e, d=kvsb[:, PF * ec:PF * (ec + 1)],
                  s=xb[p][:, 0:PF]: e.copy(d, s),
                  reads=[B(f"xb{p}")], writes=[B("kvsb")])

        for (w_d, w_idx) in ((wq_d, 0), (wk_d, 1)):
            for ec in range(8):
                p = ec % XBD
                dma(xb[p][:], w_d[T * ec:T * (ec + 1), :], f"YL{p}",
                    writes=[B(f"xb{p}")])
                pl.op("S", lambda e, d=qptm[:, D * ec:D * (ec + 1)],
                      s=xb[p][:]: e.copy(d, s),
                      reads=[B(f"xb{p}")], writes=[B("qpf0")])
            for pb in range(2):
                for dh in range(2):
                    for ec in range(8):
                        mm(pf[2 * pb + dh][:],
                           kvsb[:, PF * ec + T * pb:PF * ec + T * (pb + 1)],
                           qptm[:, D * ec + 512 * dh:D * ec + 512 * (dh + 1)],
                           start=(ec == 0), stop=(ec == 7),
                           reads=[B("kvsb"), B("qpf0")],
                           writes=[b_pf[2 * pb + dh]])
            for pb in range(2):
                for dh in range(2):
                    vcopy("V", xqb[pb][:, 512 * dh:512 * (dh + 1)],
                          pf[2 * pb + dh][:],
                          reads=[b_pf[2 * pb + dh]], writes=[B(f"xqb{pb}")])
            for pb in range(2):
                pt, bpt = trans_group(
                    lambda k, pb=pb: xqb[pb][:, T * k:T * (k + 1)], 8,
                    [B(f"xqb{pb}")])
                dview = Wqkp[:].rearrange("p (dc c) -> p dc c", c=2 * PF)[
                    :, :, PF * w_idx + T * pb:PF * w_idx + T * (pb + 1)]
                sview = pt[:].rearrange("p (dc c) -> p dc c", c=T)
                vcopy("V", dview, sview, reads=[bpt], writes=[B("Wqkp")])

        for dc in range(8):
            p = dc % XBD
            dma(xb[p][:], wv_d[T * dc:T * (dc + 1), :], f"YL{p}",
                writes=[B(f"xb{p}")])
            pl.op("S", lambda e, d=qptm[:, D * dc:D * (dc + 1)],
                  s=xb[p][:]: e.copy(d, s),
                  reads=[B(f"xb{p}")], writes=[B("qpf0")])
        for ec in range(8):
            pt, bpt = trans_group(
                lambda k, ec=ec: qptm[:, D * k + T * ec:D * k + T * (ec + 1)],
                8, [B("qpf0")])
            vcopy("V", WvT[:, D * ec:D * (ec + 1)], pt[:],
                  reads=[bpt], writes=[B("WvT")])

        def emit_quant(t, src_ap, src_buf, scale_ap, bias_ap,
                       biasm_ap=None, biasm_buf=None, cast_eng="V"):
            p = t % PAR
            pi = t % len(xqi)
            pl.op("S", lambda e, o=xqi[pi][:], i=src_ap, b=bias_ap, sc=scale_ap:
                  e.activation(o, i, AF.Identity, bias=b, scale=sc),
                  reads=[src_buf, B(f"st_a_s{p}"), B(f"st_b_s{p}")],
                  writes=[B(f"xqi{pi}")], cost=int(300 + D * 0.85))
            if not SIMFIX:
                if cast_eng == "S":
                    scopy(xqc[p][:], xqi[pi][:], reads=[B(f"xqi{pi}")],
                          writes=[B(f"xqc{p}")])
                else:
                    vcopy("V", xqc[p][:], xqi[pi][:], reads=[B(f"xqi{pi}")],
                          writes=[B(f"xqc{p}")])
                return
            p2 = t % 2
            pl.op("S", lambda e, o=tcl[p2][:], i=src_ap, b=biasm_ap,
                  sc=scale_ap:
                  e.activation(o, i, AF.Identity, bias=b, scale=sc),
                  reads=[src_buf, B(f"st_a_s{p}"), biasm_buf],
                  writes=[B(f"tcl{p2}")], cost=int(300 + D * 0.85))
            vcopy("V", xqb[p][:], xqi[pi][:], reads=[B(f"xqi{pi}")],
                  writes=[B(f"xqb{p}")])
            ts("V", sg[p2][:], xqb[p][:], -1.0, None, ALU.add,
               reads=[B(f"xqb{p}")], writes=[B(f"sg{p2}")])
            tt("V", eu[p2][:], tcl[p2][:], xqb[p][:], ALU.is_gt,
               reads=[B(f"tcl{p2}"), B(f"xqb{p}")], writes=[B(f"eu{p2}")])
            tt("V", ed[p2][:], tcl[p2][:], sg[p2][:], ALU.is_lt,
               reads=[B(f"tcl{p2}"), B(f"sg{p2}")], writes=[B(f"ed{p2}")])
            tt("V", xqc[p][:], xqb[p][:], eu[p2][:], ALU.add,
               reads=[B(f"xqb{p}"), B(f"eu{p2}")], writes=[B(f"xqc{p}")])
            tt("V", xqc[p][:], xqc[p][:], ed[p2][:], ALU.subtract,
               reads=[B(f"xqc{p}"), B(f"ed{p2}")], writes=[B(f"xqc{p}")])

        for oc in range(8):
            p = oc % PAR
            pb = oc % XBD
            dma(xb[pb][:], wf_d[T * oc:T * (oc + 1), :], f"YL{pb}",
                writes=[B(f"xb{pb}")])
            vreduce(st["mx"][p][:], xb[pb][:], ALU.max,
                    reads=[B(f"xb{pb}")], writes=[B(f"st_mx{p}")])
            vreduce(st["mn"][p][:], xb[pb][:], ALU.min,
                    reads=[B(f"xb{pb}")], writes=[B(f"st_mn{p}")])
            ts("V", st["tb"][p][:], st["mn"][p][:], -1.0, None, ALU.mult,
               reads=[B(f"st_mn{p}")], writes=[B(f"st_tb{p}")])
            tt("V", st["ta"][p][:], st["mx"][p][:], st["tb"][p][:], ALU.max,
               reads=[B(f"st_mx{p}"), B(f"st_tb{p}")], writes=[B(f"st_ta{p}")])
            ts("V", st["amax"][p][:], st["ta"][p][:], 1.0, 1e-8, ALU.mult,
               ALU.max, reads=[B(f"st_ta{p}")], writes=[B(f"st_amax{p}")])
            recip(st["inv"][p][:], st["amax"][p][:],
                  reads=[B(f"st_amax{p}")], writes=[B(f"st_inv{p}")])
            ts("V", st["a_s"][p][:], st["inv"][p][:], 127.0, None, ALU.mult,
               reads=[B(f"st_inv{p}")], writes=[B(f"st_a_s{p}")])
            pl.op("V", lambda e, o=st["b_s"][p][:]: e.memset(o, 0.0),
                  writes=[B(f"st_b_s{p}")])
            ts("V", st["xsh"][p][:], st["amax"][p][:], 1.0 / 127.0, None,
               ALU.mult, reads=[B(f"st_amax{p}")], writes=[B(f"st_xsh{p}")])
            emit_quant(oc, xb[pb][:], B(f"xb{pb}"), st["a_s"][p][:], 0.0,
                       biasm_ap=(cnh[:] if SIMFIX else None),
                       biasm_buf=(B("cnh") if SIMFIX else None))
            act(xqb[p][:], xqc[p][:], AF.Identity, scale=st["xsh"][p][:],
                reads=[B(f"xqc{p}"), B(f"st_xsh{p}")], writes=[B(f"xqb{p}")])
            pt, bpt = trans_group(
                lambda k: xqb[p][:, T * k:T * (k + 1)], 8, [B(f"xqb{p}")])
            dview = wqT[:].rearrange("p (dc c) -> p dc c", c=D)[:, :, T * oc:T * (oc + 1)]
            sview = pt[:].rearrange("p (dc c) -> p dc c", c=T)
            vcopy("V", dview, sview, reads=[bpt], writes=[B("wqT")])

        dma(brow, bi_d, "YB", writes=[B("brow")])
        ts("V", brow2, brow, 0.5, None, ALU.mult,
           reads=[B("brow")], writes=[B("xc0")])
        for nh in range(2):
            mm(pf[nh][:], onesf[:], brow2[:, 512 * nh:512 * (nh + 1)],
               start=True, stop=True,
               reads=[B("onesf"), B("xc0")], writes=[b_pf[nh]])
            vcopy("V", biash[:, 512 * nh:512 * (nh + 1)], pf[nh][:],
                  reads=[b_pf[nh]], writes=[B("biash")])

        def emit_stats_quant(p, src, src_buf, have_s1=False):
            vreduce(st["mx"][p][:], src, ALU.max,
                    reads=[src_buf], writes=[B(f"st_mx{p}")])
            vreduce(st["mn"][p][:], src, ALU.min,
                    reads=[src_buf], writes=[B(f"st_mn{p}")])
            if not have_s1:
                act(xd[0][:], src, AF.Identity, accum=st["s1"][p][:],
                    reads=[src_buf], writes=[B("xd0"), B(f"st_s1{p}")])
            act(dsq[:, 0:512], src[:, 0:512], AF.Square,
                accum=st["sa"][p][:],
                reads=[src_buf], writes=[B(dsqB), B(f"st_sa{p}")])
            act(dsq[:, 0:512], src[:, 512:1024], AF.Square,
                accum=st["sb"][p][:],
                reads=[src_buf], writes=[B(dsqB), B(f"st_sb{p}")])
            tt("V", st["s2"][p][:], st["sa"][p][:], st["sb"][p][:], ALU.add,
               reads=[B(f"st_sa{p}"), B(f"st_sb{p}")], writes=[B(f"st_s2{p}")])
            ts("V", st["m"][p][:], st["s1"][p][:], INV_D, None, ALU.mult,
               reads=[B(f"st_s1{p}")], writes=[B(f"st_m{p}")])
            tt("V", st["m2t"][p][:], st["m"][p][:], st["m"][p][:], ALU.mult,
               reads=[B(f"st_m{p}")], writes=[B(f"st_m2t{p}")])
            stt("V", st["var"][p][:], st["s2"][p][:], INV_D, st["m2t"][p][:],
                ALU.mult, ALU.subtract,
                reads=[B(f"st_s2{p}"), B(f"st_m2t{p}")], writes=[B(f"st_var{p}")])
            act(st["lnv"][p][:], st["var"][p][:], AF.Ln, bias=ceps[:],
                reads=[B(f"st_var{p}"), B("ceps")], writes=[B(f"st_lnv{p}")])
            act(st["rstd"][p][:], st["lnv"][p][:], AF.Exp, scale=-0.5,
                reads=[B(f"st_lnv{p}")], writes=[B(f"st_rstd{p}")])
            tt("V", st["ta"][p][:], st["mx"][p][:], st["m"][p][:], ALU.subtract,
               reads=[B(f"st_mx{p}"), B(f"st_m{p}")], writes=[B(f"st_ta{p}")])
            tt("V", st["tb"][p][:], st["m"][p][:], st["mn"][p][:], ALU.subtract,
               reads=[B(f"st_m{p}"), B(f"st_mn{p}")], writes=[B(f"st_tb{p}")])
            tt("V", st["tc"][p][:], st["ta"][p][:], st["tb"][p][:], ALU.max,
               reads=[B(f"st_ta{p}"), B(f"st_tb{p}")], writes=[B(f"st_tc{p}")])
            ts("V", st["amax"][p][:], st["tc"][p][:], st["rstd"][p][:], 1e-8,
               ALU.mult, ALU.max,
               reads=[B(f"st_tc{p}"), B(f"st_rstd{p}")], writes=[B(f"st_amax{p}")])
            recip(st["inv"][p][:], st["amax"][p][:],
                  reads=[B(f"st_amax{p}")], writes=[B(f"st_inv{p}")])
            ts("V", st["a_s"][p][:], st["inv"][p][:], 127.0, st["rstd"][p][:],
               ALU.mult, ALU.mult,
               reads=[B(f"st_inv{p}"), B(f"st_rstd{p}")], writes=[B(f"st_a_s{p}")])
            ts("V", st["b_s"][p][:], st["m"][p][:], st["a_s"][p][:], -1.0,
               ALU.mult, ALU.mult,
               reads=[B(f"st_m{p}"), B(f"st_a_s{p}")], writes=[B(f"st_b_s{p}")])
            ts("V", st["xsh"][p][:], st["amax"][p][:], 0.5 / 127.0, None,
               ALU.mult, reads=[B(f"st_amax{p}")], writes=[B(f"st_xsh{p}")])
            if SIMFIX:
                ts("V", st["b_m"][p][:], st["b_s"][p][:], -0.5, None, ALU.add,
                   reads=[B(f"st_b_s{p}")], writes=[B(f"st_b_m{p}")])

        def ffn_pf(t):
            return ((2 * t) % 4, (2 * t + 1) % 4)

        def emit_trans_ffn(t, evac_eng="S"):
            p = t % PAR
            p2 = t % 2
            pt, bpt = trans_group(
                lambda k: xqc[p][:, T * k:T * (k + 1)], 8, [B(f"xqc{p}")])
            if evac_eng == "S":
                scopy(xqf[p2][:], pt[:], reads=[bpt], writes=[B(f"xqf{p2}")])
            else:
                vcopy("V", xqf[p2][:], pt[:], reads=[bpt],
                      writes=[B(f"xqf{p2}")])
            j0, j1 = ffn_pf(t)
            for nh, j in ((0, j0), (1, j1)):
                for c in range(8):
                    mm(pf[j][:], xqf[p2][:, T * c:T * (c + 1)],
                       wqT[:, D * c + 512 * nh:D * c + 512 * (nh + 1)],
                       start=(c == 0), stop=(c == 7),
                       reads=[B(f"xqf{p2}"), B("wqT")], writes=[b_pf[j]])

        def emit_epilogue(t, p, src_biased, src_biased_buf, dst, dst_buf):
            acc = (st["ta"][p], st["tb"][p])
            accb = (B(f"st_ta{p}"), B(f"st_tb{p}"))
            j0, j1 = ffn_pf(t)
            for nh, j in ((0, j0), (1, j1)):
                pl.op("V", lambda e, o=dst[:, 512 * nh:512 * (nh + 1)],
                      i0=pf[j][:], sc=st["xsh"][p][:],
                      i1=src_biased[:, 512 * nh:512 * (nh + 1)],
                      a=acc[nh][:]:
                      e.scalar_tensor_tensor(o, i0, sc, i1, op0=ALU.mult,
                                             op1=ALU.add, accum_out=a),
                      reads=[b_pf[j], B(f"st_xsh{p}"), src_biased_buf],
                      writes=[dst_buf, accb[nh]], cost=int(180 + 512 * 1.3))
            tt("V", st["s1"][p][:], st["ta"][p][:], st["tb"][p][:], ALU.add,
               reads=[accb[0], accb[1]], writes=[B(f"st_s1{p}")])

        def emit_load(t, dram, tag):
            px = t % XBD
            reads = [B(f"spill{t % 2}")] if tag == "s" else []
            dma(xb[px][:], dram[T * t:T * (t + 1), :], f"YL{px}",
                reads=reads, writes=[B(f"xb{px}")])

        def stage_a1(t):
            p = t % PAR
            px = t % XBD
            emit_stats_quant(p, xb[px][:], B(f"xb{px}"))
            emit_quant(t, xb[px][:], B(f"xb{px}"), st["a_s"][p][:],
                       st["b_s"][p][:], biasm_ap=st["b_m"][p][:],
                       biasm_buf=B(f"st_b_m{p}"))

        def stage_b1(t):
            p = t % PAR
            px = t % XBD
            emit_trans_ffn(t, evac_eng="V")
            tt("V", xc[p][:], xb[px][:], biash[:], ALU.add,
               reads=[B(f"xb{px}"), B("biash")], writes=[B(f"xc{p}")])
            emit_epilogue(t, p, xc[p][:], B(f"xc{p}"), xb[px][:], B(f"xb{px}"))
            dma(spill_d[T * t:T * (t + 1), :], xb[px][:], f"YS{t % 2}",
                reads=[B(f"xb{px}")], writes=[B(f"spill{t % 2}")])

        def stage_c1(t):
            p = t % PAR
            p2 = t % 2
            px = t % XBD
            act(dsq[:, 0:512], xb[px][:, 0:512], AF.Square,
                accum=st["sa"][p][:],
                reads=[B(f"xb{px}")], writes=[B(dsqB), B(f"st_sa{p}")])
            act(dsq[:, 0:512], xb[px][:, 512:1024], AF.Square,
                accum=st["sb"][p][:],
                reads=[B(f"xb{px}")], writes=[B(dsqB), B(f"st_sb{p}")])
            tt("V", st["s2"][p][:], st["sa"][p][:], st["sb"][p][:], ALU.add,
               reads=[B(f"st_sa{p}"), B(f"st_sb{p}")], writes=[B(f"st_s2{p}")])
            ts("V", st["m"][p][:], st["s1"][p][:], INV_D, None, ALU.mult,
               reads=[B(f"st_s1{p}")], writes=[B(f"st_m{p}")])
            tt("V", st["m2t"][p][:], st["m"][p][:], st["m"][p][:], ALU.mult,
               reads=[B(f"st_m{p}")], writes=[B(f"st_m2t{p}")])
            stt("V", st["var"][p][:], st["s2"][p][:], INV_D, st["m2t"][p][:],
                ALU.mult, ALU.subtract,
                reads=[B(f"st_s2{p}"), B(f"st_m2t{p}")],
                writes=[B(f"st_var{p}")])
            act(st["lnv"][p][:], st["var"][p][:], AF.Ln, bias=ceps[:],
                reads=[B(f"st_var{p}"), B("ceps")], writes=[B(f"st_lnv{p}")])
            act(st["rstd"][p][:], st["lnv"][p][:], AF.Exp, scale=-0.5,
                reads=[B(f"st_lnv{p}")], writes=[B(f"st_rstd{p}")])
            ts("V", st["b_s"][p][:], st["m"][p][:], st["rstd"][p][:], -1.0,
               ALU.mult, ALU.mult,
               reads=[B(f"st_m{p}"), B(f"st_rstd{p}")], writes=[B(f"st_b_s{p}")])
            act(xn2[p2][:], xb[px][:], AF.Identity, bias=st["b_s"][p][:],
                scale=st["rstd"][p][:],
                reads=[B(f"xb{px}"), B(f"st_b_s{p}"), B(f"st_rstd{p}")],
                writes=[B(f"xn2{p2}")])
            pt, bpt = trans_group(
                lambda k: xn2[p2][:, T * k:T * (k + 1)], 8, [B(f"xn2{p2}")])
            scopy(xn2f[p2][:], pt[:], reads=[bpt], writes=[B(f"xn2f{p2}")])
            jq = t % 2
            for c in range(8):
                mm(pqk[jq][:, 0:2 * PF], xn2f[p2][:, T * c:T * (c + 1)],
                   Wqkp[:, 2 * PF * c:2 * PF * (c + 1)],
                   start=(c == 0), stop=(c == 7),
                   reads=[B(f"xn2f{p2}"), B("Wqkp")], writes=[b_pqk[jq]])
            act(ef[p2][:], pqk[jq][:, 0:2 * PF], AF.Exp,
                reads=[b_pqk[jq]], writes=[B(f"ef{p2}")])
            ts("V", ef[p2][:], ef[p2][:], 1.0, None, ALU.min,
               reads=[B(f"ef{p2}")], writes=[B(f"ef{p2}")])
            pl.op("V", lambda e, o=qpb[p2][:], i0=pqk[jq][:, 0:PF],
                  i1=ef[p2][:, 0:PF], a=st["zsum"][p][:]:
                  e.scalar_tensor_tensor(o, i0, 0.0, i1, op0=ALU.max,
                                         op1=ALU.add, accum_out=a),
                  reads=[b_pqk[jq], B(f"ef{p2}")],
                  writes=[B(f"qpb{p2}"), B(f"st_zsum{p}")],
                  cost=int(180 + PF * 1.3))
            stt("V", kpr[:, PF * t:PF * (t + 1)], pqk[jq][:, PF:2 * PF],
                0.0, ef[p2][:, PF:2 * PF], ALU.max, ALU.add,
                reads=[b_pqk[jq], B(f"ef{p2}")],
                writes=[B("kpr")])
            ts("V", st["zse"][p][:], st["zsum"][p][:], EPS_Z, None, ALU.add,
               reads=[B(f"st_zsum{p}")], writes=[B(f"st_zse{p}")])
            recip(zst[:, t:t + 1], st["zse"][p][:],
                  reads=[B(f"st_zse{p}")], writes=[B("zst")])
            dma(qptm[:, PF * t:PF * (t + 1)], qpb[p2][:], f"YQ{p2}",
                reads=[B(f"qpb{p2}")], writes=[B("qpf0")])
            dma(xn2sp_d[T * t:T * (t + 1), :], xn2[p2][:], f"YN{t % 2}",
                reads=[B(f"xn2{p2}")], writes=[B(f"xn2sp_t{t}")])

        emit_load(0, x_d, "x")
        for t in range(nt):
            stage_a1(t)
            if t > 0:
                stage_c1(t - 1)
            if t + 1 < nt:
                emit_load(t + 1, x_d, "x")
            stage_b1(t)
        stage_c1(nt - 1)

        if SIMFIX:
            rl_bufs = [xn2[0], xn2[1], xn2f[0], xn2f[1]]
            rl_names = ["xn20", "xn21", "xn2f0", "xn2f1"]
            rl_streams = ["YN0", "YN1", "YN2", "YN3"]
        else:
            rl_bufs = [xn2[0], xn2[1], xn2f[0], xn2f[1], xqf[0], xqf[1],
                       dump_v]
            rl_names = ["xn20", "xn21", "xn2f0", "xn2f1", "xqf0", "xqf1",
                        "dump_v"]
            rl_streams = ["YN0", "YN1", "YN2", "YN3", "YK0", "YK1", "YK2"]
        nrl = len(rl_bufs)
        for t in range(nt):
            rb = t % nrl
            dma(rl_bufs[rb][:], xn2sp_d[T * t:T * (t + 1), :], rl_streams[rb],
                reads=[B(f"xn2sp_t{t}")], writes=[B(rl_names[rb])])
            for pc in range(2):
                for nh in range(2):
                    mm(pf[2 * pc + nh][:],
                       kpr[:, PF * t + T * pc:PF * t + T * (pc + 1)],
                       rl_bufs[rb][:, 512 * nh:512 * (nh + 1)],
                       start=(t == 0), stop=(t == nt - 1),
                       reads=[B("kpr"), B(rl_names[rb])],
                       writes=[b_pf[2 * pc + nh]])

        for pc in range(2):
            for nh in range(2):
                vcopy("V", xqf[pc][:, 512 * nh:512 * (nh + 1)],
                      pf[2 * pc + nh][:],
                      reads=[b_pf[2 * pc + nh]], writes=[B(f"xqf{pc}")])
        for pc in range(2):
            pt, bpt = trans_group(
                lambda k, pc=pc: xqf[pc][:, T * k:T * (k + 1)], 8,
                [B(f"xqf{pc}")])
            for j in range(2):
                dview = xn2[j][:].rearrange("p (k c) -> p k c", c=PF)[:, :, T * pc:T * (pc + 1)]
                sview = pt[:, 512 * j:512 * (j + 1)].rearrange("p (k c) -> p k c", c=T)
                vcopy("V", dview, sview, reads=[bpt], writes=[B(f"xn2{j}")])
        for pc2 in range(2):
            for dh in range(2):
                for ec in range(8):
                    lhs = xn2[ec // 4][:, PF * (ec % 4) + T * pc2:
                                       PF * (ec % 4) + T * (pc2 + 1)]
                    mm(pqk[dh][:], lhs,
                       WvT[:, D * ec + 512 * dh:D * ec + 512 * (dh + 1)],
                       start=(ec == 0), stop=(ec == 7),
                       reads=[B(f"xn2{ec // 4}"), B("WvT")], writes=[b_pqk[dh]])
                vcopy("V", kvsb[:, D * pc2 + 512 * dh:D * pc2 + 512 * (dh + 1)],
                      pqk[dh][:], reads=[b_pqk[dh]], writes=[B("kvsb")])

        store_ops = []

        def emit_qp_trans(t):
            r = t % 4
            base = qpfr_off + 2 * T * r
            pt, bpt = trans_group(
                lambda k: qptm[:, PF * t + T * k:PF * t + T * (k + 1)], 2,
                [B("qpf0")])
            scopy(qptm[:, base:base + 2 * T], pt[:, 0:2 * T],
                  reads=[bpt], writes=[B(f"qpfr{r}")])

        def stage_n2(t):
            p = t % PAR
            px = t % XBD
            r = t % 4
            base = qpfr_off + 2 * T * r
            for nh in range(2):
                for pc in range(2):
                    mm(pqk[nh][:], qptm[:, base + T * pc:base + T * (pc + 1)],
                       kvsb[:, D * pc + 512 * nh:D * pc + 512 * (nh + 1)],
                       start=(pc == 0), stop=(pc == 1),
                       reads=[B(f"qpfr{r}"), B("kvsb")], writes=[b_pqk[nh]])
            acc = (st["ta"][p], st["tb"][p])
            accb = (B(f"st_ta{p}"), B(f"st_tb{p}"))
            for nh in range(2):
                pl.op("V", lambda e, o=xc[p][:, 512 * nh:512 * (nh + 1)],
                      i0=pqk[nh][:], sc=zst[:, t:t + 1],
                      i1=xb[px][:, 512 * nh:512 * (nh + 1)], a=acc[nh][:]:
                      e.scalar_tensor_tensor(o, i0, sc, i1, op0=ALU.mult,
                                             op1=ALU.add, accum_out=a),
                      reads=[b_pqk[nh], B("zst"), B(f"xb{px}")],
                      writes=[B(f"xc{p}"), accb[nh]], cost=int(180 + 512 * 1.3))
            tt("V", st["s1"][p][:], st["ta"][p][:], st["tb"][p][:], ALU.add,
               reads=[accb[0], accb[1]], writes=[B(f"st_s1{p}")])

        def stage_a2(t):
            p = t % PAR
            emit_stats_quant(p, xc[p][:], B(f"xc{p}"), have_s1=True)
            emit_quant(t, xc[p][:], B(f"xc{p}"), st["a_s"][p][:],
                       st["b_s"][p][:], biasm_ap=st["b_m"][p][:],
                       biasm_buf=B(f"st_b_m{p}"), cast_eng="S")

        def stage_b2(t):
            p = t % PAR
            p2 = t % 2
            emit_trans_ffn(t, evac_eng="S")
            tt("V", xd[0][:], xc[p][:], biash[:], ALU.add,
               reads=[B(f"xc{p}"), B("biash")], writes=[B("xd0")])
            emit_epilogue(t, p, xd[0][:], B("xd0"), xc[p][:], B(f"xc{p}"))
            store_ops.append(
                dma(out_d[T * t:T * (t + 1), :], xc[p][:], f"YO{p}",
                    reads=[B(f"xc{p}")]))

        emit_load(0, spill_d, "s")
        for t in range(min(4, nt)):
            emit_qp_trans(t)
        for t in range(nt):
            stage_n2(t)
            if t + 4 < nt:
                emit_qp_trans(t + 4)
            stage_a2(t)
            if t + 1 < nt:
                emit_load(t + 1, spill_d, "s")
            if t > 0:
                stage_b2(t - 1)
        stage_b2(nt - 1)

        pl.op("Y", lambda e: e.nop(), extra=store_ops, sem_key="YO0")

        import os as _os
        if _os.environ.get("KSCHED", "1") == "1":
            pl.schedule()
        pl.finalize()
        with nc.Block() as block:
            @block.vector
            def _(eng):
                pl.emit_engine("V", eng, sems)

            @block.scalar
            def _(eng):
                pl.emit_engine("S", eng, sems)

            @block.gpsimd
            def _(eng):
                pl.emit_engine("G", eng, sems)

            @block.tensor
            def _(eng):
                pl.emit_engine("T", eng, sems)

            @block.sync
            def _(eng):
                pl.emit_engine("Y", eng, sems)

    return nc


def kernel(L0_attn_x, L0_attn_Wq, L0_attn_Wk, L0_attn_Wv, L0_attn_proj,
           L0_ffn_weight, L0_ffn_bias):
    x = np.ascontiguousarray(np.asarray(L0_attn_x, dtype=np.float32))
    wq = np.ascontiguousarray(np.asarray(L0_attn_Wq, dtype=np.float32))
    wk = np.ascontiguousarray(np.asarray(L0_attn_Wk, dtype=np.float32))
    wv = np.ascontiguousarray(np.asarray(L0_attn_Wv, dtype=np.float32))
    pj = np.ascontiguousarray(np.asarray(L0_attn_proj, dtype=np.float32))
    wf = np.ascontiguousarray(np.asarray(L0_ffn_weight, dtype=np.float32))
    bi = np.ascontiguousarray(
        np.asarray(L0_ffn_bias, dtype=np.float32).reshape(1, D))
    nc = build_nc()
    in_maps = [{"x": x[b], "wq": wq, "wk": wk, "wv": wv, "proj": pj,
                "wf": wf, "bias": bi} for b in range(NCORES)]
    res = run_bass_kernel_spmd(nc, in_maps, core_ids=list(range(NCORES)))
    return np.stack([np.asarray(r["out"], dtype=np.float32)
                     for r in res.results], axis=0)


# version 42
# speedup vs baseline: 1.0371x; 1.0087x over previous
import os
import numpy as np
import concourse.bass as bass
import concourse.mybir as mybir
from concourse.bass_utils import run_bass_kernel_spmd
from contextlib import ExitStack

F32 = mybir.dt.float32
BF16 = mybir.dt.bfloat16
I32 = mybir.dt.int32
I16 = mybir.dt.int16
AF = mybir.ActivationFunctionType
ALU = mybir.AluOpType

NCORES = 8
SEQ = 8192
D = 1024
PF = 256
T = 128
NT = SEQ // T
EPS_LN = 1e-5
EPS_Z = 1e-8
PAR = 3
PARANOID = True

ENGS = ("V", "S", "G", "T", "Y")


class _Op:
    __slots__ = ("engine", "emit", "waits", "semval", "is_dma", "sem_key",
                 "idx", "cost", "lat")

    def __init__(self, engine, emit, waits, is_dma, sem_key, idx, cost, lat):
        self.engine = engine
        self.emit = emit
        self.waits = waits
        self.semval = None
        self.is_dma = is_dma
        self.sem_key = sem_key
        self.idx = idx
        self.cost = cost
        self.lat = lat


class _Buf:
    __slots__ = ("name", "last_write", "readers")

    def __init__(self, name):
        self.name = name
        self.last_write = None
        self.readers = []


class Plan:
    def __init__(self):
        self.ops = {e: [] for e in ENGS}
        self.bufs = {}
        self.nops = 0

    def buf(self, name):
        b = self.bufs.get(name)
        if b is None:
            b = _Buf(name)
            self.bufs[name] = b
        return b

    def op(self, engine, emit, reads=(), writes=(), extra=(), sem_key=None,
           cost=None, lat=None, dma=False):
        waits = set()
        for b in reads:
            if b.last_write is not None:
                waits.add(b.last_write)
        for b in writes:
            waits.update(b.readers)
            if b.last_write is not None:
                waits.add(b.last_write)
        for o in extra:
            if o is not None:
                waits.add(o)
        is_dma = dma or engine == "Y"
        if cost is None:
            cost = 300 if is_dma else 400
        if lat is None:
            lat = 2500 if is_dma else 200
        o = _Op(engine, emit, waits, is_dma, sem_key or engine,
                self.nops, cost, lat)
        self.nops += 1
        self.ops[engine].append(o)
        for b in reads:
            b.readers.append(o)
        for b in writes:
            b.last_write = o
            b.readers = []
        return o

    def schedule(self):
        import heapq
        allops = sorted((o for el in self.ops.values() for o in el),
                        key=lambda o: o.idx)
        succs = {o: [] for o in allops}
        indeg = {o: 0 for o in allops}
        for o in allops:
            for w in o.waits:
                succs[w].append(o)
                indeg[o] += 1
        pr = {}
        for o in reversed(allops):
            m = 0
            for t in succs[o]:
                if pr[t] > m:
                    m = pr[t]
            pr[o] = m + o.cost + o.lat
        engine_free = {e: 0 for e in self.ops}
        ready = {e: [] for e in self.ops}
        completions = []
        order = {e: [] for e in self.ops}
        import os as _os
        sched_engs = set(_os.environ.get("KENG", "VSGTY"))

        def _key(o):
            if o.engine[0] not in sched_engs:
                return (o.idx, o.idx, o)
            return (-pr[o], o.idx, o)
        for o in allops:
            if indeg[o] == 0:
                heapq.heappush(ready[o.engine], _key(o))
        ndone = 0
        now = 0
        while ndone < len(allops):
            progressed = True
            while progressed:
                progressed = False
                for e in self.ops:
                    if engine_free[e] <= now and ready[e]:
                        _, _, o = heapq.heappop(ready[e])
                        start = now
                        engine_free[e] = start + o.cost
                        heapq.heappush(completions,
                                       (start + o.cost + o.lat, o.idx, o))
                        order[e].append(o)
                        ndone += 1
                        progressed = True
            cand = []
            if completions:
                cand.append(completions[0][0])
            for e in self.ops:
                if ready[e] and engine_free[e] > now:
                    cand.append(engine_free[e])
            if not cand:
                if not completions:
                    break
                cand.append(completions[0][0])
            now = min(cand)
            while completions and completions[0][0] <= now:
                _, _, o = heapq.heappop(completions)
                for t in succs[o]:
                    indeg[t] -= 1
                    if indeg[t] == 0:
                        heapq.heappush(ready[t.engine], _key(t))
        assert ndone == len(allops), f"scheduler stuck: {ndone}/{len(allops)}"
        self.ops = order
        return now

    def _needs_sem(self, producer, consumer_engine):
        if producer.engine == "T" and consumer_engine == "T":
            return False
        return (producer.engine != consumer_engine or producer.is_dma
                or PARANOID)

    def transitive_reduce(self):
        spos = {}
        for e, el in self.ops.items():
            for k, o in enumerate(el):
                spos[o] = k
        order = []
        for el in self.ops.values():
            order.extend(el)
        indeg = {o: 0 for o in order}
        succs = {o: [] for o in order}
        for o in order:
            for w in o.waits:
                succs[w].append(o)
                indeg[o] += 1
        from collections import deque
        q = deque(o for o in order if indeg[o] == 0)
        wm = {}
        topo = []
        while q:
            o = q.popleft()
            topo.append(o)
            for t in succs[o]:
                indeg[t] -= 1
                if indeg[t] == 0:
                    q.append(t)
        assert len(topo) == len(order), "reduce topo stuck"
        for o in topo:
            m = {}
            kept = []
            for w in sorted(o.waits, key=lambda w: -spos[w]):
                lane = (w.engine, w.sem_key)
                if m.get(lane, -1) >= spos[w]:
                    continue
                kept.append(w)
                wmw = wm.get(w)
                if wmw:
                    for k2, v2 in wmw.items():
                        if m.get(k2, -1) < v2:
                            m[k2] = v2
                if m.get(lane, -1) < spos[w]:
                    m[lane] = spos[w]
            o.waits = set(kept)
            mo = dict(m)
            mo[(o.engine, o.sem_key)] = spos[o]
            wm[o] = mo

    def finalize(self):
        self.transitive_reduce()
        needed = set()
        for elist in self.ops.values():
            for o in elist:
                for w in o.waits:
                    if self._needs_sem(w, o.engine):
                        needed.add(w)
        ctr = {}
        for elist in self.ops.values():
            for o in elist:
                if o in needed or o.is_dma:
                    v = ctr.get(o.sem_key, 0) + (16 if o.is_dma else 1)
                    ctr[o.sem_key] = v
                    o.semval = v

    def emit_engine(self, ename, eng, sems):
        waited = {}
        for o in self.ops[ename]:
            by = {}
            for w in o.waits:
                if w.semval is None or not self._needs_sem(w, ename):
                    continue
                if by.get(w.sem_key, 0) < w.semval:
                    by[w.sem_key] = w.semval
            for k2 in sorted(by):
                v = by[k2]
                if waited.get(k2, 0) >= v:
                    continue
                eng.wait_ge(sems[k2], v)
                waited[k2] = v
            inst = o.emit(eng)
            if o.semval is not None:
                inst.then_inc(sems[o.sem_key], 16 if o.is_dma else 1)


def build_nc(nt=NT):
    nc = bass.Bass("TRN2", target_bir_lowering=False, debug=False)
    seq = nt * T
    x_d = nc.dram_tensor("x", [seq, D], F32, kind="ExternalInput").ap()
    wq_d = nc.dram_tensor("wq", [D, D], F32, kind="ExternalInput").ap()
    wk_d = nc.dram_tensor("wk", [D, D], F32, kind="ExternalInput").ap()
    wv_d = nc.dram_tensor("wv", [D, D], F32, kind="ExternalInput").ap()
    pj_d = nc.dram_tensor("proj", [D, PF], F32, kind="ExternalInput").ap()
    wf_d = nc.dram_tensor("wf", [D, D], F32, kind="ExternalInput").ap()
    bi_d = nc.dram_tensor("bias", [1, D], F32, kind="ExternalInput").ap()
    out_d = nc.dram_tensor("out", [seq, D], F32, kind="ExternalOutput").ap()
    spill_d = nc.dram_tensor("spill", [seq, D], F32, kind="Internal").ap()
    xn2sp_d = nc.dram_tensor("xn2sp", [seq, D], BF16, kind="Internal").ap()

    pl = Plan()
    B = pl.buf

    with ExitStack() as ctx:
        def sb(name, shape, dt=F32):
            return ctx.enter_context(nc.sbuf_tensor(name, shape, dt))

        def psum(name, shape, dt=F32):
            return ctx.enter_context(nc.psum_tensor(name, shape, dt))

        wqT = sb("wqT", [T, 8 * D], BF16)
        Wqkp = sb("Wqkp", [T, 8 * 2 * PF], BF16)
        WvT = sb("WvT", [T, 8 * D], BF16)
        biash = sb("biash", [T, D], F32)
        qptm = sb("qptm", [T, max(PF * nt, 8 * D) + 4 * 2 * T], BF16)
        qpfr_off = max(PF * nt, 8 * D)
        zst = sb("zst", [T, nt], F32)
        kvsb = sb("kvsb", [T, 2 * D], BF16)
        identb = sb("identb", [T, T], BF16)
        onesf = sb("onesf", [1, T], F32)
        ceps = sb("ceps", [T, 1], F32)

        def pair(name, shape, dt=F32, d=None):
            return [sb(f"{name}{i}", shape, dt) for i in range(d or PAR)]

        SIMFIX = os.environ.get("BASS_SIM_FP64", "0") == "1"
        XBD = 2 if SIMFIX else max(PAR, 4)
        xb = pair("xb", [T, D], d=XBD)
        xc = pair("xc", [T, D])
        xd = pair("xd", [T, D], d=1)
        brow = xd[0][0:1, :]
        yq = pair("yq", [T, D])
        xqi = pair("xqi", [T, D], I32, d=(1 if SIMFIX else 2))
        xqb = pair("xqb", [T, D], BF16)
        xqc = pair("xqc", [T, D], BF16)
        xqf = pair("xqf", [T, D], BF16, d=2)
        xn2 = pair("xn2", [T, D], BF16, d=2)
        xn2f = pair("xn2f", [T, D], BF16, d=2)
        if SIMFIX:
            tcl = pair("tcl", [T, D], BF16, d=2)
            sg = pair("sg", [T, D], BF16, d=2)
            eu = pair("eu", [T, D], BF16, d=2)
            ed = pair("ed", [T, D], BF16, d=2)
            cnh = sb("cnh", [T, 1], F32)
        else:
            dump_v = sb("dump_v", [T, D], BF16)
            dump_s = sb("dump_s", [T, 512])
        dsq = xd[0] if SIMFIX else dump_s
        dsqB = "xd0" if SIMFIX else "dump_s"
        brow2 = xc[0][0:1, :]
        ef = pair("ef", [T, 2 * PF], d=2)
        qpb = pair("qpb", [T, PF], BF16, d=2)
        kpr = sb("kpr", [T, PF * nt], BF16)
        qksb = pair("qksb", [T, 2 * PF], d=2)
        stat_names = ("mx", "mn", "s1", "s2", "m", "m2t", "var", "lnv", "rstd",
                      "ta", "tb", "tc", "sa", "sb", "amax", "inv", "a_s", "b_s", "xsh",
                      "zsum", "zse", "b_m")
        st = {n: pair(f"st_{n}", [T, 1]) for n in stat_names}

        pf = [psum(f"pf{i}", [T, 512], F32) for i in range(4)]
        ptrs = [psum(f"ptr{i}", [T, D], BF16) for i in range(2)]
        pqk = [psum(f"pqk{i}", [T, 512], F32) for i in range(2)]

        sems = {}
        sem_names = ["V", "S", "G", "T", "YB"]
        sem_names += [f"YO{i}" for i in range(PAR)]
        sem_names += [f"YL{i}" for i in range(max(PAR, 4))]
        sem_names += [f"YS{i}" for i in range(2)]
        sem_names += [f"YN{i}" for i in range(4)]
        sem_names += [f"YK{i}" for i in range(3)]
        sem_names += [f"YQ{i}" for i in range(2)]

        for e in sem_names:
            sems[e] = ctx.enter_context(nc.semaphore(f"sem_{e}"))

        b_pf = [B(f"pf{i}") for i in range(4)]
        b_pqk = [B(f"pqk{i}") for i in range(2)]
        b_ptrs = [B(f"ptr{i}") for i in range(2)]
        ptr_ctr = [0]

        def next_ptr():
            i = ptr_ctr[0] & 1
            ptr_ctr[0] += 1
            return ptrs[i], b_ptrs[i]

        INV_D = 1.0 / D

        def _free(ap):
            n = 1
            for d in ap.shape[1:]:
                n *= d
            return n

        def _is16(ap):
            return mybir.dt.size(ap.dtype) == 2

        def ts(eng, out, in0, s1, s2, op0, op1=None, accum=None,
               reads=(), writes=()):
            kw = {}
            if op1 is not None:
                kw["op1"] = op1
            if accum is not None:
                kw["accum_out"] = accum
            f = _free(out)
            c = 100 + (f * 1.15 if accum is not None else
                       f * (0.3 if _is16(out) else 0.6))
            return pl.op(eng, lambda e, o=out, i=in0, a=s1, b=s2, kw=kw,
                         op0=op0: e.tensor_scalar(o, i, a, b, op0=op0, **kw),
                         reads=reads, writes=writes, cost=int(c))

        def tt(eng, out, a, b_, op, reads=(), writes=()):
            f = _free(out)
            if eng == "G":
                c = 250 + f * 2.15
            else:
                c = 160 + f * (0.6 if (_is16(out) and _is16(a)) else 1.15)
            return pl.op(eng, lambda e, o=out, x=a, y=b_, op=op:
                         e.tensor_tensor(o, x, y, op),
                         reads=reads, writes=writes, cost=int(c))

        def stt(eng, out, in0, scal, in1, op0, op1, reads=(), writes=()):
            f = _free(out)
            return pl.op(eng, lambda e, o=out, i0=in0, s=scal, i1=in1,
                         op0=op0, op1=op1:
                         e.scalar_tensor_tensor(o, i0, s, i1, op0=op0, op1=op1),
                         reads=reads, writes=writes, cost=int(180 + f * 1.3))

        def act(out, in_, func, bias=0.0, scale=1.0, accum=None,
                reads=(), writes=()):
            kw = {}
            if accum is not None:
                kw["accum_out"] = accum
            f = _free(out)
            return pl.op("S", lambda e, o=out, i=in_, f=func, b=bias, s=scale,
                         kw=kw: e.activation(o, i, f, bias=b, scale=s, **kw),
                         reads=reads, writes=writes, cost=int(300 + f * 0.85))

        def scopy(out, in_, reads=(), writes=()):
            f = _free(out)
            return pl.op("S", lambda e, o=out, i=in_: e.copy(o, i),
                         reads=reads, writes=writes, cost=int(300 + f * 0.85))

        def vcopy(eng, out, in_, reads=(), writes=()):
            f = _free(out)
            c = 130 + f * (0.55 if _is16(out) else 0.7)
            if eng == "G":
                c = 250 + f * 1.1
            return pl.op(eng, lambda e, o=out, i=in_: e.tensor_copy(o, i),
                         reads=reads, writes=writes, cost=int(c))

        def vreduce(out, in_, op, reads=(), writes=()):
            f = _free(in_)
            return pl.op("V", lambda e, o=out, i=in_, p=op:
                         e.tensor_reduce(o, i, axis=mybir.AxisListType.X, op=p),
                         reads=reads, writes=writes, cost=int(180 + f * 0.75))

        def recip(out, in_, reads=(), writes=()):
            return pl.op("V", lambda e, o=out, i=in_: e.reciprocal(o, i),
                         reads=reads, writes=writes, cost=170)

        def mm(out, lhsT, rhs, start, stop, reads=(), writes=()):
            f = _free(out)
            return pl.op("T", lambda e, o=out, l=lhsT, r=rhs, a=start, b=stop:
                         e.matmul(o, l, r, start=a, stop=b),
                         reads=reads, writes=writes, cost=int(60 + f * 0.43))

        def tr(out, in_, reads=(), writes=()):
            return pl.op("T", lambda e, o=out, i=in_:
                         e.transpose(o, i, identb[:]),
                         reads=tuple(reads) + (B("identb"),), writes=writes,
                         cost=130)

        def trans_group(src_ap_fn, n, src_reads):
            pt, bpt = next_ptr()
            for k in range(n):
                tr(pt[:, T * k:T * (k + 1)], src_ap_fn(k),
                   reads=src_reads, writes=[bpt])
            return pt, bpt

        def gdma(out, in_, stream, accum=None, reads=(), writes=()):
            kw = {} if accum is None else {"accum_op": accum}
            f = _free(out)
            return pl.op("G", lambda e, o=out, i=in_, kw=kw:
                         e.dma_start(out=o, in_=i, **kw),
                         reads=reads, writes=writes, sem_key=stream,
                         cost=500, lat=500 + int(f * 3.4), dma=True)

        def dma(out, in_, stream, reads=(), writes=()):
            f = _free(out)
            return pl.op("Y", lambda e, o=out, i=in_: e.dma_start(out=o, in_=i),
                         reads=reads, writes=writes, sem_key=stream,
                         cost=250, lat=800 + int(f * 5.1))

        pl.op("G", lambda e: e.memset(identb[:], 0.0), writes=[B("identb")])
        pl.op("G", lambda e: e.affine_select(
            out=identb[:], in_=identb[:], compare_op=ALU.not_equal, fill=1.0,
            base=0, pattern=[[-1, T]], channel_multiplier=1),
            reads=[B("identb")], writes=[B("identb")])
        pl.op("G", lambda e: e.memset(ceps[:], EPS_LN), writes=[B("ceps")])
        if SIMFIX:
            pl.op("G", lambda e: e.memset(cnh[:], -0.5), writes=[B("cnh")])
        pl.op("G", lambda e: e.memset(onesf[:], 1.0), writes=[B("onesf")])

        for ec in range(8):
            p = ec % XBD
            dma(xb[p][:, 0:PF], pj_d[T * ec:T * (ec + 1), :], f"YL{p}",
                writes=[B(f"xb{p}")])
            pl.op("S", lambda e, d=kvsb[:, PF * ec:PF * (ec + 1)],
                  s=xb[p][:, 0:PF]: e.copy(d, s),
                  reads=[B(f"xb{p}")], writes=[B("kvsb")])

        for (w_d, w_idx) in ((wq_d, 0), (wk_d, 1)):
            for ec in range(8):
                p = ec % XBD
                dma(xb[p][:], w_d[T * ec:T * (ec + 1), :], f"YL{p}",
                    writes=[B(f"xb{p}")])
                pl.op("S", lambda e, d=qptm[:, D * ec:D * (ec + 1)],
                      s=xb[p][:]: e.copy(d, s),
                      reads=[B(f"xb{p}")], writes=[B("qpf0")])
            for pb in range(2):
                for dh in range(2):
                    for ec in range(8):
                        mm(pf[2 * pb + dh][:],
                           kvsb[:, PF * ec + T * pb:PF * ec + T * (pb + 1)],
                           qptm[:, D * ec + 512 * dh:D * ec + 512 * (dh + 1)],
                           start=(ec == 0), stop=(ec == 7),
                           reads=[B("kvsb"), B("qpf0")],
                           writes=[b_pf[2 * pb + dh]])
            for pb in range(2):
                for dh in range(2):
                    vcopy("V", xqb[pb][:, 512 * dh:512 * (dh + 1)],
                          pf[2 * pb + dh][:],
                          reads=[b_pf[2 * pb + dh]], writes=[B(f"xqb{pb}")])
            for pb in range(2):
                pt, bpt = trans_group(
                    lambda k, pb=pb: xqb[pb][:, T * k:T * (k + 1)], 8,
                    [B(f"xqb{pb}")])
                dview = Wqkp[:].rearrange("p (dc c) -> p dc c", c=2 * PF)[
                    :, :, PF * w_idx + T * pb:PF * w_idx + T * (pb + 1)]
                sview = pt[:].rearrange("p (dc c) -> p dc c", c=T)
                vcopy("V", dview, sview, reads=[bpt], writes=[B("Wqkp")])

        for dc in range(8):
            p = dc % XBD
            dma(xb[p][:], wv_d[T * dc:T * (dc + 1), :], f"YL{p}",
                writes=[B(f"xb{p}")])
            pl.op("S", lambda e, d=qptm[:, D * dc:D * (dc + 1)],
                  s=xb[p][:]: e.copy(d, s),
                  reads=[B(f"xb{p}")], writes=[B("qpf0")])
        for ec in range(8):
            pt, bpt = trans_group(
                lambda k, ec=ec: qptm[:, D * k + T * ec:D * k + T * (ec + 1)],
                8, [B("qpf0")])
            vcopy("V", WvT[:, D * ec:D * (ec + 1)], pt[:],
                  reads=[bpt], writes=[B("WvT")])

        def emit_quant(t, src_ap, src_buf, scale_ap, bias_ap,
                       biasm_ap=None, biasm_buf=None, cast_eng="V"):
            p = t % PAR
            pi = t % len(xqi)
            pl.op("S", lambda e, o=xqi[pi][:], i=src_ap, b=bias_ap, sc=scale_ap:
                  e.activation(o, i, AF.Identity, bias=b, scale=sc),
                  reads=[src_buf, B(f"st_a_s{p}"), B(f"st_b_s{p}")],
                  writes=[B(f"xqi{pi}")], cost=int(300 + D * 0.85))
            if not SIMFIX:
                if cast_eng == "S":
                    scopy(xqc[p][:], xqi[pi][:], reads=[B(f"xqi{pi}")],
                          writes=[B(f"xqc{p}")])
                else:
                    vcopy("V", xqc[p][:], xqi[pi][:], reads=[B(f"xqi{pi}")],
                          writes=[B(f"xqc{p}")])
                return
            p2 = t % 2
            pl.op("S", lambda e, o=tcl[p2][:], i=src_ap, b=biasm_ap,
                  sc=scale_ap:
                  e.activation(o, i, AF.Identity, bias=b, scale=sc),
                  reads=[src_buf, B(f"st_a_s{p}"), biasm_buf],
                  writes=[B(f"tcl{p2}")], cost=int(300 + D * 0.85))
            vcopy("V", xqb[p][:], xqi[pi][:], reads=[B(f"xqi{pi}")],
                  writes=[B(f"xqb{p}")])
            ts("V", sg[p2][:], xqb[p][:], -1.0, None, ALU.add,
               reads=[B(f"xqb{p}")], writes=[B(f"sg{p2}")])
            tt("V", eu[p2][:], tcl[p2][:], xqb[p][:], ALU.is_gt,
               reads=[B(f"tcl{p2}"), B(f"xqb{p}")], writes=[B(f"eu{p2}")])
            tt("V", ed[p2][:], tcl[p2][:], sg[p2][:], ALU.is_lt,
               reads=[B(f"tcl{p2}"), B(f"sg{p2}")], writes=[B(f"ed{p2}")])
            tt("V", xqc[p][:], xqb[p][:], eu[p2][:], ALU.add,
               reads=[B(f"xqb{p}"), B(f"eu{p2}")], writes=[B(f"xqc{p}")])
            tt("V", xqc[p][:], xqc[p][:], ed[p2][:], ALU.subtract,
               reads=[B(f"xqc{p}"), B(f"ed{p2}")], writes=[B(f"xqc{p}")])

        for oc in range(8):
            p = oc % PAR
            pb = oc % XBD
            dma(xb[pb][:], wf_d[T * oc:T * (oc + 1), :], f"YL{pb}",
                writes=[B(f"xb{pb}")])
            vreduce(st["mx"][p][:], xb[pb][:], ALU.max,
                    reads=[B(f"xb{pb}")], writes=[B(f"st_mx{p}")])
            vreduce(st["mn"][p][:], xb[pb][:], ALU.min,
                    reads=[B(f"xb{pb}")], writes=[B(f"st_mn{p}")])
            ts("V", st["tb"][p][:], st["mn"][p][:], -1.0, None, ALU.mult,
               reads=[B(f"st_mn{p}")], writes=[B(f"st_tb{p}")])
            tt("V", st["ta"][p][:], st["mx"][p][:], st["tb"][p][:], ALU.max,
               reads=[B(f"st_mx{p}"), B(f"st_tb{p}")], writes=[B(f"st_ta{p}")])
            ts("V", st["amax"][p][:], st["ta"][p][:], 1.0, 1e-8, ALU.mult,
               ALU.max, reads=[B(f"st_ta{p}")], writes=[B(f"st_amax{p}")])
            recip(st["inv"][p][:], st["amax"][p][:],
                  reads=[B(f"st_amax{p}")], writes=[B(f"st_inv{p}")])
            ts("V", st["a_s"][p][:], st["inv"][p][:], 127.0, None, ALU.mult,
               reads=[B(f"st_inv{p}")], writes=[B(f"st_a_s{p}")])
            pl.op("V", lambda e, o=st["b_s"][p][:]: e.memset(o, 0.0),
                  writes=[B(f"st_b_s{p}")])
            ts("V", st["xsh"][p][:], st["amax"][p][:], 1.0 / 127.0, None,
               ALU.mult, reads=[B(f"st_amax{p}")], writes=[B(f"st_xsh{p}")])
            emit_quant(oc, xb[pb][:], B(f"xb{pb}"), st["a_s"][p][:], 0.0,
                       biasm_ap=(cnh[:] if SIMFIX else None),
                       biasm_buf=(B("cnh") if SIMFIX else None))
            act(xqb[p][:], xqc[p][:], AF.Identity, scale=st["xsh"][p][:],
                reads=[B(f"xqc{p}"), B(f"st_xsh{p}")], writes=[B(f"xqb{p}")])
            pt, bpt = trans_group(
                lambda k: xqb[p][:, T * k:T * (k + 1)], 8, [B(f"xqb{p}")])
            dview = wqT[:].rearrange("p (dc c) -> p dc c", c=D)[:, :, T * oc:T * (oc + 1)]
            sview = pt[:].rearrange("p (dc c) -> p dc c", c=T)
            vcopy("V", dview, sview, reads=[bpt], writes=[B("wqT")])

        dma(brow, bi_d, "YB", writes=[B("brow")])
        ts("V", brow2, brow, 0.5, None, ALU.mult,
           reads=[B("brow")], writes=[B("xc0")])
        for nh in range(2):
            mm(pf[nh][:], onesf[:], brow2[:, 512 * nh:512 * (nh + 1)],
               start=True, stop=True,
               reads=[B("onesf"), B("xc0")], writes=[b_pf[nh]])
            vcopy("V", biash[:, 512 * nh:512 * (nh + 1)], pf[nh][:],
                  reads=[b_pf[nh]], writes=[B("biash")])

        def emit_stats_quant(p, src, src_buf, have_s1=False):
            vreduce(st["mx"][p][:], src, ALU.max,
                    reads=[src_buf], writes=[B(f"st_mx{p}")])
            vreduce(st["mn"][p][:], src, ALU.min,
                    reads=[src_buf], writes=[B(f"st_mn{p}")])
            if not have_s1:
                act(xd[0][:], src, AF.Identity, accum=st["s1"][p][:],
                    reads=[src_buf], writes=[B("xd0"), B(f"st_s1{p}")])
            act(dsq[:, 0:512], src[:, 0:512], AF.Square,
                accum=st["sa"][p][:],
                reads=[src_buf], writes=[B(dsqB), B(f"st_sa{p}")])
            act(dsq[:, 0:512], src[:, 512:1024], AF.Square,
                accum=st["sb"][p][:],
                reads=[src_buf], writes=[B(dsqB), B(f"st_sb{p}")])
            tt("V", st["s2"][p][:], st["sa"][p][:], st["sb"][p][:], ALU.add,
               reads=[B(f"st_sa{p}"), B(f"st_sb{p}")], writes=[B(f"st_s2{p}")])
            ts("V", st["m"][p][:], st["s1"][p][:], INV_D, None, ALU.mult,
               reads=[B(f"st_s1{p}")], writes=[B(f"st_m{p}")])
            tt("V", st["m2t"][p][:], st["m"][p][:], st["m"][p][:], ALU.mult,
               reads=[B(f"st_m{p}")], writes=[B(f"st_m2t{p}")])
            stt("V", st["var"][p][:], st["s2"][p][:], INV_D, st["m2t"][p][:],
                ALU.mult, ALU.subtract,
                reads=[B(f"st_s2{p}"), B(f"st_m2t{p}")], writes=[B(f"st_var{p}")])
            act(st["lnv"][p][:], st["var"][p][:], AF.Ln, bias=ceps[:],
                reads=[B(f"st_var{p}"), B("ceps")], writes=[B(f"st_lnv{p}")])
            act(st["rstd"][p][:], st["lnv"][p][:], AF.Exp, scale=-0.5,
                reads=[B(f"st_lnv{p}")], writes=[B(f"st_rstd{p}")])
            tt("V", st["ta"][p][:], st["mx"][p][:], st["m"][p][:], ALU.subtract,
               reads=[B(f"st_mx{p}"), B(f"st_m{p}")], writes=[B(f"st_ta{p}")])
            tt("V", st["tb"][p][:], st["m"][p][:], st["mn"][p][:], ALU.subtract,
               reads=[B(f"st_m{p}"), B(f"st_mn{p}")], writes=[B(f"st_tb{p}")])
            tt("V", st["tc"][p][:], st["ta"][p][:], st["tb"][p][:], ALU.max,
               reads=[B(f"st_ta{p}"), B(f"st_tb{p}")], writes=[B(f"st_tc{p}")])
            ts("V", st["amax"][p][:], st["tc"][p][:], st["rstd"][p][:], 1e-8,
               ALU.mult, ALU.max,
               reads=[B(f"st_tc{p}"), B(f"st_rstd{p}")], writes=[B(f"st_amax{p}")])
            recip(st["inv"][p][:], st["amax"][p][:],
                  reads=[B(f"st_amax{p}")], writes=[B(f"st_inv{p}")])
            ts("V", st["a_s"][p][:], st["inv"][p][:], 127.0, st["rstd"][p][:],
               ALU.mult, ALU.mult,
               reads=[B(f"st_inv{p}"), B(f"st_rstd{p}")], writes=[B(f"st_a_s{p}")])
            ts("V", st["b_s"][p][:], st["m"][p][:], st["a_s"][p][:], -1.0,
               ALU.mult, ALU.mult,
               reads=[B(f"st_m{p}"), B(f"st_a_s{p}")], writes=[B(f"st_b_s{p}")])
            ts("V", st["xsh"][p][:], st["amax"][p][:], 0.5 / 127.0, None,
               ALU.mult, reads=[B(f"st_amax{p}")], writes=[B(f"st_xsh{p}")])
            if SIMFIX:
                ts("V", st["b_m"][p][:], st["b_s"][p][:], -0.5, None, ALU.add,
                   reads=[B(f"st_b_s{p}")], writes=[B(f"st_b_m{p}")])

        def ffn_pf(t):
            return ((2 * t) % 4, (2 * t + 1) % 4)

        def emit_trans_ffn(t, evac_eng="S"):
            p = t % PAR
            p2 = t % 2
            pt, bpt = trans_group(
                lambda k: xqc[p][:, T * k:T * (k + 1)], 8, [B(f"xqc{p}")])
            if evac_eng == "S":
                scopy(xqf[p2][:], pt[:], reads=[bpt], writes=[B(f"xqf{p2}")])
            else:
                vcopy("V", xqf[p2][:], pt[:], reads=[bpt],
                      writes=[B(f"xqf{p2}")])
            j0, j1 = ffn_pf(t)
            for nh, j in ((0, j0), (1, j1)):
                for c in range(8):
                    mm(pf[j][:], xqf[p2][:, T * c:T * (c + 1)],
                       wqT[:, D * c + 512 * nh:D * c + 512 * (nh + 1)],
                       start=(c == 0), stop=(c == 7),
                       reads=[B(f"xqf{p2}"), B("wqT")], writes=[b_pf[j]])

        def emit_epilogue(t, p, src_biased, src_biased_buf, dst, dst_buf):
            acc = (st["ta"][p], st["tb"][p])
            accb = (B(f"st_ta{p}"), B(f"st_tb{p}"))
            j0, j1 = ffn_pf(t)
            for nh, j in ((0, j0), (1, j1)):
                pl.op("V", lambda e, o=dst[:, 512 * nh:512 * (nh + 1)],
                      i0=pf[j][:], sc=st["xsh"][p][:],
                      i1=src_biased[:, 512 * nh:512 * (nh + 1)],
                      a=acc[nh][:]:
                      e.scalar_tensor_tensor(o, i0, sc, i1, op0=ALU.mult,
                                             op1=ALU.add, accum_out=a),
                      reads=[b_pf[j], B(f"st_xsh{p}"), src_biased_buf],
                      writes=[dst_buf, accb[nh]], cost=int(180 + 512 * 1.3))
            tt("V", st["s1"][p][:], st["ta"][p][:], st["tb"][p][:], ALU.add,
               reads=[accb[0], accb[1]], writes=[B(f"st_s1{p}")])

        def emit_load(t, dram, tag):
            px = t % XBD
            reads = [B(f"spill{t % 2}")] if tag == "s" else []
            dma(xb[px][:], dram[T * t:T * (t + 1), :], f"YL{px}",
                reads=reads, writes=[B(f"xb{px}")])

        def stage_a1(t):
            p = t % PAR
            px = t % XBD
            emit_stats_quant(p, xb[px][:], B(f"xb{px}"))
            emit_quant(t, xb[px][:], B(f"xb{px}"), st["a_s"][p][:],
                       st["b_s"][p][:], biasm_ap=st["b_m"][p][:],
                       biasm_buf=B(f"st_b_m{p}"))

        def stage_b1(t):
            p = t % PAR
            px = t % XBD
            emit_trans_ffn(t, evac_eng="V")
            tt("V", xc[p][:], xb[px][:], biash[:], ALU.add,
               reads=[B(f"xb{px}"), B("biash")], writes=[B(f"xc{p}")])
            emit_epilogue(t, p, xc[p][:], B(f"xc{p}"), xb[px][:], B(f"xb{px}"))
            dma(spill_d[T * t:T * (t + 1), :], xb[px][:], f"YS{t % 2}",
                reads=[B(f"xb{px}")], writes=[B(f"spill{t % 2}")])

        def stage_c1(t):
            p = t % PAR
            p2 = t % 2
            px = t % XBD
            act(dsq[:, 0:512], xb[px][:, 0:512], AF.Square,
                accum=st["sa"][p][:],
                reads=[B(f"xb{px}")], writes=[B(dsqB), B(f"st_sa{p}")])
            act(dsq[:, 0:512], xb[px][:, 512:1024], AF.Square,
                accum=st["sb"][p][:],
                reads=[B(f"xb{px}")], writes=[B(dsqB), B(f"st_sb{p}")])
            tt("V", st["s2"][p][:], st["sa"][p][:], st["sb"][p][:], ALU.add,
               reads=[B(f"st_sa{p}"), B(f"st_sb{p}")], writes=[B(f"st_s2{p}")])
            ts("V", st["m"][p][:], st["s1"][p][:], INV_D, None, ALU.mult,
               reads=[B(f"st_s1{p}")], writes=[B(f"st_m{p}")])
            tt("V", st["m2t"][p][:], st["m"][p][:], st["m"][p][:], ALU.mult,
               reads=[B(f"st_m{p}")], writes=[B(f"st_m2t{p}")])
            stt("V", st["var"][p][:], st["s2"][p][:], INV_D, st["m2t"][p][:],
                ALU.mult, ALU.subtract,
                reads=[B(f"st_s2{p}"), B(f"st_m2t{p}")],
                writes=[B(f"st_var{p}")])
            act(st["lnv"][p][:], st["var"][p][:], AF.Ln, bias=ceps[:],
                reads=[B(f"st_var{p}"), B("ceps")], writes=[B(f"st_lnv{p}")])
            act(st["rstd"][p][:], st["lnv"][p][:], AF.Exp, scale=-0.5,
                reads=[B(f"st_lnv{p}")], writes=[B(f"st_rstd{p}")])
            ts("V", st["b_s"][p][:], st["m"][p][:], st["rstd"][p][:], -1.0,
               ALU.mult, ALU.mult,
               reads=[B(f"st_m{p}"), B(f"st_rstd{p}")], writes=[B(f"st_b_s{p}")])
            act(xn2[p2][:], xb[px][:], AF.Identity, bias=st["b_s"][p][:],
                scale=st["rstd"][p][:],
                reads=[B(f"xb{px}"), B(f"st_b_s{p}"), B(f"st_rstd{p}")],
                writes=[B(f"xn2{p2}")])
            pt, bpt = trans_group(
                lambda k: xn2[p2][:, T * k:T * (k + 1)], 8, [B(f"xn2{p2}")])
            scopy(xn2f[p2][:], pt[:], reads=[bpt], writes=[B(f"xn2f{p2}")])
            jq = t % 2
            for c in range(8):
                mm(pqk[jq][:, 0:2 * PF], xn2f[p2][:, T * c:T * (c + 1)],
                   Wqkp[:, 2 * PF * c:2 * PF * (c + 1)],
                   start=(c == 0), stop=(c == 7),
                   reads=[B(f"xn2f{p2}"), B("Wqkp")], writes=[b_pqk[jq]])
            scopy(qksb[p2][:], pqk[jq][:, 0:2 * PF],
                  reads=[b_pqk[jq]], writes=[B(f"qksb{p2}")])
            act(ef[p2][:], qksb[p2][:, 0:2 * PF], AF.Exp,
                reads=[B(f"qksb{p2}")], writes=[B(f"ef{p2}")])
            ts("V", ef[p2][:], ef[p2][:], 1.0, None, ALU.min,
               reads=[B(f"ef{p2}")], writes=[B(f"ef{p2}")])
            pl.op("V", lambda e, o=qpb[p2][:], i0=qksb[p2][:, 0:PF],
                  i1=ef[p2][:, 0:PF], a=st["zsum"][p][:]:
                  e.scalar_tensor_tensor(o, i0, 0.0, i1, op0=ALU.max,
                                         op1=ALU.add, accum_out=a),
                  reads=[B(f"qksb{p2}"), B(f"ef{p2}")],
                  writes=[B(f"qpb{p2}"), B(f"st_zsum{p}")],
                  cost=int(180 + PF * 1.3))
            stt("V", kpr[:, PF * t:PF * (t + 1)], qksb[p2][:, PF:2 * PF],
                0.0, ef[p2][:, PF:2 * PF], ALU.max, ALU.add,
                reads=[B(f"qksb{p2}"), B(f"ef{p2}")],
                writes=[B("kpr")])
            ts("V", st["zse"][p][:], st["zsum"][p][:], EPS_Z, None, ALU.add,
               reads=[B(f"st_zsum{p}")], writes=[B(f"st_zse{p}")])
            recip(zst[:, t:t + 1], st["zse"][p][:],
                  reads=[B(f"st_zse{p}")], writes=[B("zst")])
            dma(qptm[:, PF * t:PF * (t + 1)], qpb[p2][:], f"YQ{p2}",
                reads=[B(f"qpb{p2}")], writes=[B("qpf0")])
            dma(xn2sp_d[T * t:T * (t + 1), :], xn2[p2][:], f"YN{t % 2}",
                reads=[B(f"xn2{p2}")], writes=[B(f"xn2sp_t{t}")])

        emit_load(0, x_d, "x")
        for t in range(nt):
            stage_a1(t)
            if t > 0:
                stage_c1(t - 1)
            if t + 1 < nt:
                emit_load(t + 1, x_d, "x")
            stage_b1(t)
        stage_c1(nt - 1)

        if SIMFIX:
            rl_bufs = [xn2[0], xn2[1], xn2f[0], xn2f[1]]
            rl_names = ["xn20", "xn21", "xn2f0", "xn2f1"]
            rl_streams = ["YN0", "YN1", "YN2", "YN3"]
        else:
            rl_bufs = [xn2[0], xn2[1], xn2f[0], xn2f[1], xqf[0], xqf[1],
                       dump_v]
            rl_names = ["xn20", "xn21", "xn2f0", "xn2f1", "xqf0", "xqf1",
                        "dump_v"]
            rl_streams = ["YN0", "YN1", "YN2", "YN3", "YK0", "YK1", "YK2"]
        nrl = len(rl_bufs)
        for t in range(nt):
            rb = t % nrl
            dma(rl_bufs[rb][:], xn2sp_d[T * t:T * (t + 1), :], rl_streams[rb],
                reads=[B(f"xn2sp_t{t}")], writes=[B(rl_names[rb])])
            for pc in range(2):
                for nh in range(2):
                    mm(pf[2 * pc + nh][:],
                       kpr[:, PF * t + T * pc:PF * t + T * (pc + 1)],
                       rl_bufs[rb][:, 512 * nh:512 * (nh + 1)],
                       start=(t == 0), stop=(t == nt - 1),
                       reads=[B("kpr"), B(rl_names[rb])],
                       writes=[b_pf[2 * pc + nh]])

        for pc in range(2):
            for nh in range(2):
                vcopy("V", xqf[pc][:, 512 * nh:512 * (nh + 1)],
                      pf[2 * pc + nh][:],
                      reads=[b_pf[2 * pc + nh]], writes=[B(f"xqf{pc}")])
        for pc in range(2):
            pt, bpt = trans_group(
                lambda k, pc=pc: xqf[pc][:, T * k:T * (k + 1)], 8,
                [B(f"xqf{pc}")])
            for j in range(2):
                dview = xn2[j][:].rearrange("p (k c) -> p k c", c=PF)[:, :, T * pc:T * (pc + 1)]
                sview = pt[:, 512 * j:512 * (j + 1)].rearrange("p (k c) -> p k c", c=T)
                vcopy("V", dview, sview, reads=[bpt], writes=[B(f"xn2{j}")])
        for pc2 in range(2):
            for dh in range(2):
                for ec in range(8):
                    lhs = xn2[ec // 4][:, PF * (ec % 4) + T * pc2:
                                       PF * (ec % 4) + T * (pc2 + 1)]
                    mm(pqk[dh][:], lhs,
                       WvT[:, D * ec + 512 * dh:D * ec + 512 * (dh + 1)],
                       start=(ec == 0), stop=(ec == 7),
                       reads=[B(f"xn2{ec // 4}"), B("WvT")], writes=[b_pqk[dh]])
                vcopy("V", kvsb[:, D * pc2 + 512 * dh:D * pc2 + 512 * (dh + 1)],
                      pqk[dh][:], reads=[b_pqk[dh]], writes=[B("kvsb")])

        store_ops = []

        def emit_qp_trans(t):
            r = t % 4
            base = qpfr_off + 2 * T * r
            pt, bpt = trans_group(
                lambda k: qptm[:, PF * t + T * k:PF * t + T * (k + 1)], 2,
                [B("qpf0")])
            scopy(qptm[:, base:base + 2 * T], pt[:, 0:2 * T],
                  reads=[bpt], writes=[B(f"qpfr{r}")])

        def stage_n2(t):
            p = t % PAR
            px = t % XBD
            r = t % 4
            base = qpfr_off + 2 * T * r
            for nh in range(2):
                for pc in range(2):
                    mm(pqk[nh][:], qptm[:, base + T * pc:base + T * (pc + 1)],
                       kvsb[:, D * pc + 512 * nh:D * pc + 512 * (nh + 1)],
                       start=(pc == 0), stop=(pc == 1),
                       reads=[B(f"qpfr{r}"), B("kvsb")], writes=[b_pqk[nh]])
            acc = (st["ta"][p], st["tb"][p])
            accb = (B(f"st_ta{p}"), B(f"st_tb{p}"))
            for nh in range(2):
                pl.op("V", lambda e, o=xc[p][:, 512 * nh:512 * (nh + 1)],
                      i0=pqk[nh][:], sc=zst[:, t:t + 1],
                      i1=xb[px][:, 512 * nh:512 * (nh + 1)], a=acc[nh][:]:
                      e.scalar_tensor_tensor(o, i0, sc, i1, op0=ALU.mult,
                                             op1=ALU.add, accum_out=a),
                      reads=[b_pqk[nh], B("zst"), B(f"xb{px}")],
                      writes=[B(f"xc{p}"), accb[nh]], cost=int(180 + 512 * 1.3))
            tt("V", st["s1"][p][:], st["ta"][p][:], st["tb"][p][:], ALU.add,
               reads=[accb[0], accb[1]], writes=[B(f"st_s1{p}")])

        def stage_a2(t):
            p = t % PAR
            emit_stats_quant(p, xc[p][:], B(f"xc{p}"), have_s1=True)
            emit_quant(t, xc[p][:], B(f"xc{p}"), st["a_s"][p][:],
                       st["b_s"][p][:], biasm_ap=st["b_m"][p][:],
                       biasm_buf=B(f"st_b_m{p}"), cast_eng="S")

        def stage_b2(t):
            p = t % PAR
            p2 = t % 2
            emit_trans_ffn(t, evac_eng="S")
            tt("V", xd[0][:], xc[p][:], biash[:], ALU.add,
               reads=[B(f"xc{p}"), B("biash")], writes=[B("xd0")])
            emit_epilogue(t, p, xd[0][:], B("xd0"), xc[p][:], B(f"xc{p}"))
            store_ops.append(
                dma(out_d[T * t:T * (t + 1), :], xc[p][:], f"YO{p}",
                    reads=[B(f"xc{p}")]))

        emit_load(0, spill_d, "s")
        for t in range(min(4, nt)):
            emit_qp_trans(t)
        for t in range(nt):
            stage_n2(t)
            if t + 4 < nt:
                emit_qp_trans(t + 4)
            stage_a2(t)
            if t + 1 < nt:
                emit_load(t + 1, spill_d, "s")
            if t > 0:
                stage_b2(t - 1)
        stage_b2(nt - 1)

        pl.op("Y", lambda e: e.nop(), extra=store_ops, sem_key="YO0")

        import os as _os
        if _os.environ.get("KSCHED", "1") == "1":
            pl.schedule()
        pl.finalize()
        with nc.Block() as block:
            @block.vector
            def _(eng):
                pl.emit_engine("V", eng, sems)

            @block.scalar
            def _(eng):
                pl.emit_engine("S", eng, sems)

            @block.gpsimd
            def _(eng):
                pl.emit_engine("G", eng, sems)

            @block.tensor
            def _(eng):
                pl.emit_engine("T", eng, sems)

            @block.sync
            def _(eng):
                pl.emit_engine("Y", eng, sems)

    return nc


def kernel(L0_attn_x, L0_attn_Wq, L0_attn_Wk, L0_attn_Wv, L0_attn_proj,
           L0_ffn_weight, L0_ffn_bias):
    x = np.ascontiguousarray(np.asarray(L0_attn_x, dtype=np.float32))
    wq = np.ascontiguousarray(np.asarray(L0_attn_Wq, dtype=np.float32))
    wk = np.ascontiguousarray(np.asarray(L0_attn_Wk, dtype=np.float32))
    wv = np.ascontiguousarray(np.asarray(L0_attn_Wv, dtype=np.float32))
    pj = np.ascontiguousarray(np.asarray(L0_attn_proj, dtype=np.float32))
    wf = np.ascontiguousarray(np.asarray(L0_ffn_weight, dtype=np.float32))
    bi = np.ascontiguousarray(
        np.asarray(L0_ffn_bias, dtype=np.float32).reshape(1, D))
    nc = build_nc()
    in_maps = [{"x": x[b], "wq": wq, "wk": wk, "wv": wv, "proj": pj,
                "wf": wf, "bias": bi} for b in range(NCORES)]
    res = run_bass_kernel_spmd(nc, in_maps, core_ids=list(range(NCORES)))
    return np.stack([np.asarray(r["out"], dtype=np.float32)
                     for r in res.results], axis=0)
